# version 07
# baseline (speedup 1.0000x reference)
import contextlib
import numpy as np
import ml_dtypes

import concourse.bass as bass
import concourse.mybir as mybir
from concourse.bass_utils import run_bass_kernel_spmd

F32 = mybir.dt.float32
BF16 = mybir.dt.bfloat16
AF = mybir.ActivationFunctionType
ALU = mybir.AluOpType
AX = mybir.AxisListType
ET = mybir.EngineType

NCORES = 8
D = 2048
T = 1024
TH = 512
KD = 16
NEXP = 32
NH, NKV, HD = 32, 4, 128
SCALING = HD ** -0.5
EPS_LN = 1e-5
EPS_RMS = 1e-6
NEG = -1e30

ENGMAP = {'T': ET.PE, 'V': ET.DVE, 'A': ET.Activation, 'P': ET.Pool, 'S': ET.SP}


def rk(t):
    return t if isinstance(t, str) else t.name


class Plan:

    def __init__(self):
        self.cl = {e: [] for e in ENGMAP}
        self.eng_sem = {e: 0 for e in ENGMAP}
        self.dma_sems = {}
        self.lastw = {}
        self.readers = {}
        self.wm = {e: {} for e in ENGMAP}
        self._sem_handles = {}

    def new_dma_sem(self, name):
        self.dma_sems[name] = 0

    def _emit_wait(self, e, sem_name, val):
        if val <= 0 or self.wm[e].get(sem_name, 0) >= val:
            return
        self.wm[e][sem_name] = val

        def f(eng, plan=self, sem_name=sem_name, val=val):
            eng.wait_ge(plan._sem_handles[sem_name], val)
        self.cl[e].append(f)

    def _dep_waits(self, e, reads, writes):
        need = {}
        for r in reads:
            for sname, c in self.lastw.get(r, {}).items():
                need[sname] = max(need.get(sname, 0), c)
        for w in writes:
            for sname, c in self.lastw.get(w, {}).items():
                need[sname] = max(need.get(sname, 0), c)
            for sname, c in self.readers.get(w, {}).items():
                need[sname] = max(need.get(sname, 0), c)
        for sname in sorted(need):
            val = need[sname]
            if sname in self.dma_sems:
                val = max(val, self.dma_sems[sname])
            self._emit_wait(e, sname, val)

    def op(self, e, reads=(), writes=(), emit=None, inc=1):
        reads = [rk(r) for r in reads]
        writes = [rk(w) for w in writes]
        self._dep_waits(e, reads, writes)
        self.eng_sem[e] += inc
        cnt = self.eng_sem[e]

        def f(eng, plan=self, emit=emit, inc=inc, e=e):
            ins = emit(eng)
            ins.then_inc(plan._sem_handles[e], inc)
        self.cl[e].append(f)
        for w in writes:
            self.lastw[w] = {e: cnt}
            self.readers[w] = {}
        for r in reads:
            self.readers.setdefault(r, {})[e] = cnt
        return cnt

    def dma(self, e, sem, reads=(), writes=(), emit=None):
        reads = [rk(r) for r in reads]
        writes = [rk(w) for w in writes]
        self._dep_waits(e, reads, writes)
        self.dma_sems[sem] += 16
        cnt = self.dma_sems[sem]

        def f(eng, plan=self, emit=emit, sem=sem):
            ins = emit(eng)
            ins.then_inc(plan._sem_handles[sem], 16)
        self.cl[e].append(f)
        for w in writes:
            self.lastw[w] = {sem: cnt}
            self.readers[w] = {}
        for r in reads:
            self.readers.setdefault(r, {})[sem] = cnt
        return cnt

    def raw(self, e, f):
        self.cl[e].append(f)

    def barrier(self):
        snap_e = dict(self.eng_sem)
        snap_d = dict(self.dma_sems)
        for e in ENGMAP:
            for s, v in snap_e.items():
                if s != e:
                    self._emit_wait(e, s, v)
            for s, v in snap_d.items():
                self._emit_wait(e, s, v)

    def switch(self, engines, index_fn, n, body):
        import copy
        for e in engines:
            self.cl[e].append(("SW_BEGIN", index_fn, n))
        snap = (dict(self.eng_sem), dict(self.dma_sems))
        lw_snap = copy.deepcopy(self.lastw)
        rd_snap = copy.deepcopy(self.readers)
        wm_snap = {e: dict(self.wm[e]) for e in ENGMAP}
        end_state = None
        end_lw, end_rd = [], []
        for i in range(n):
            if i > 0:
                for e in engines:
                    self.cl[e].append(("SW_NEXT",))
                self.eng_sem, self.dma_sems = dict(snap[0]), dict(snap[1])
                self.lastw = copy.deepcopy(lw_snap)
                self.readers = copy.deepcopy(rd_snap)
                for e in ENGMAP:
                    self.wm[e] = dict(wm_snap[e])
            body(i)
            st = (dict(self.eng_sem), dict(self.dma_sems))
            if end_state is None:
                end_state = st
            else:
                assert st == end_state, f"switch case {i}: sem deltas differ"
            end_lw.append(copy.deepcopy(self.lastw))
            end_rd.append(copy.deepcopy(self.readers))
        for e in engines:
            self.cl[e].append(("SW_END",))
        mlw, mrd = {}, {}
        for lw in end_lw:
            for k, d in lw.items():
                t = mlw.setdefault(k, {})
                for sn, c in d.items():
                    t[sn] = max(t.get(sn, 0), c)
        for rd in end_rd:
            for k, d in rd.items():
                t = mrd.setdefault(k, {})
                for sn, c in d.items():
                    t[sn] = max(t.get(sn, 0), c)
        self.lastw, self.readers = mlw, mrd
        for e in ENGMAP:
            self.wm[e] = dict(wm_snap[e])

    def run_engine(self, e, eng, nc):
        pid = None
        gens = []
        for item in self.cl[e]:
            if callable(item):
                item(eng)
            elif item[0] == "SW_BEGIN":
                if pid is None:
                    pid = nc.partition_id([ENGMAP[e]])
                _, index_fn, n = item
                g = iter(eng.Switch(index_fn(pid), n))
                gens.append(g)
                next(g)
            elif item[0] == "SW_NEXT":
                next(gens[-1])
            else:
                try:
                    next(gens[-1])
                    raise AssertionError("switch not exhausted")
                except StopIteration:
                    pass
                gens.pop()



def _gu_lay(w):
    return np.ascontiguousarray(
        w.reshape(16, 128, 3, 256).transpose(1, 2, 0, 3).reshape(128, 12288))


def _d_lay(w):
    return np.ascontiguousarray(
        w.reshape(6, 128, 16, 128).transpose(1, 2, 0, 3).reshape(128, 12288))


def _core_token_idx(c):
    if c % 2 == 0:
        return np.concatenate([np.arange(0, 512), np.arange(1536, 2048)])
    return np.concatenate([np.arange(512, 1024), np.arange(1024, 1536)])


def prep_host(inputs):
    x = np.asarray(inputs['L0_attn_hidden_states'])
    wq = np.asarray(inputs['L0_attn_Wq'])
    wk = np.asarray(inputs['L0_attn_Wk'])
    wv = np.asarray(inputs['L0_attn_Wv'])
    wo = np.asarray(inputs['L0_attn_Wo'])
    qnw = np.ascontiguousarray(np.asarray(inputs['L0_attn_q_norm_w']).reshape(128, 1)).astype(np.float32)
    knw = np.ascontiguousarray(np.asarray(inputs['L0_attn_k_norm_w']).reshape(128, 1)).astype(np.float32)
    rw = np.asarray(inputs['L0_ffn_router_w'])
    wg = np.asarray(inputs['L0_ffn_w_gate'])
    wu = np.asarray(inputs['L0_ffn_w_up'])
    wd = np.asarray(inputs['L0_ffn_w_down'])

    wffn = np.concatenate(
        [np.concatenate([_gu_lay(wg[e]), _gu_lay(wu[e]), _d_lay(wd[e])], 1)
         for e in range(NEXP)], 1)

    def head_lay(w, nh):
        return w.reshape(16, 128, nh, 128).transpose(1, 2, 0, 3).reshape(128, nh * 2048)
    wqkv = np.ascontiguousarray(
        np.concatenate([head_lay(wk, 4), head_lay(wv, 4), head_lay(wq, 32)], 1))

    wo_r = np.ascontiguousarray(
        wo.reshape(32, 128, 16, 128).transpose(1, 0, 2, 3)
          .reshape(128, 4, 8, 16, 128).transpose(0, 1, 3, 2, 4).reshape(128, 65536))

    rtw = np.ascontiguousarray(
        rw.T.reshape(16, 128, 32).transpose(1, 0, 2).reshape(128, 512)).astype(np.float32)

    ones_col = np.ones((128, 1), np.float32)
    ones_row = np.ones((1, 128), np.float32)
    identb = np.eye(128, dtype=np.float32).astype(ml_dtypes.bfloat16)
    identf = np.eye(128, dtype=np.float32)
    sel = np.zeros((32, 4096), np.float32)
    for e in range(32):
        sel[e, 128 * e:128 * (e + 1)] = 1.0
    sel = sel.astype(ml_dtypes.bfloat16)
    tri = np.zeros((128, 2048), np.float32)
    p = np.arange(128)[:, None]
    j = np.arange(512)[None, :]
    for qc in range(4):
        tri[:, 512 * qc:512 * (qc + 1)] = np.where(j > 128 * qc + p, NEG, 0.0)
    tri = tri.astype(ml_dtypes.bfloat16)

    inv_freq = 1.0 / (1e6 ** (np.arange(0, 128, 2, dtype=np.float64) / 128.0))

    per_core = []
    for c in range(NCORES):
        b = c // 2
        idx = _core_token_idx(c)
        xT = np.ascontiguousarray(x[b, idx, :].T).astype(np.float32)
        ang = idx[None, :].astype(np.float64) * inv_freq[:, None]
        ang = np.concatenate([ang, ang], 0)
        per_core.append(dict(
            xT=xT, wffn=wffn, wqkv=wqkv, wo=wo_r, rtw=rtw,
            cosT=np.cos(ang).astype(np.float32), sinT=np.sin(ang).astype(np.float32),
            tri=tri, qnw=qnw, knw=knw, ones_col=ones_col, ones_row=ones_row,
            ones_col_bf=ones_col.astype(ml_dtypes.bfloat16),
            identb=identb, identf=identf, sel=sel))
    return per_core


def reassemble(results):
    out = np.empty((4, 2048, 2048), np.float32)
    for c in range(NCORES):
        idx = _core_token_idx(c)
        out[c // 2, idx, :] = results[c]['y'].T
    return out



def build_program(phases=('ffn0', 'attn', 'ffn1'), n_exp=NEXP, n_heads=NH, dbg=()):
    nc = bass.Bass("TRN2", target_bir_lowering=False, debug=False)
    P = Plan()

    d_xT = nc.dram_tensor("xT", [D, T], F32, kind="ExternalInput").ap()
    d_wffn = nc.dram_tensor("wffn", [128, NEXP * 36864], F32, kind="ExternalInput").ap()
    d_wqkv = nc.dram_tensor("wqkv", [128, 40 * 2048], F32, kind="ExternalInput").ap()
    d_wo = nc.dram_tensor("wo", [128, 65536], F32, kind="ExternalInput").ap()
    d_rtw = nc.dram_tensor("rtw", [128, 512], F32, kind="ExternalInput").ap()
    d_cos = nc.dram_tensor("cosT", [128, T], F32, kind="ExternalInput").ap()
    d_sin = nc.dram_tensor("sinT", [128, T], F32, kind="ExternalInput").ap()
    d_tri = nc.dram_tensor("tri", [128, 2048], BF16, kind="ExternalInput").ap()
    d_qnw = nc.dram_tensor("qnw", [128, 1], F32, kind="ExternalInput").ap()
    d_knw = nc.dram_tensor("knw", [128, 1], F32, kind="ExternalInput").ap()
    d_onesc = nc.dram_tensor("ones_col", [128, 1], F32, kind="ExternalInput").ap()
    d_onesb = nc.dram_tensor("ones_col_bf", [128, 1], BF16, kind="ExternalInput").ap()
    d_onesr = nc.dram_tensor("ones_row", [1, 128], F32, kind="ExternalInput").ap()
    d_identb = nc.dram_tensor("identb", [128, 128], BF16, kind="ExternalInput").ap()
    d_identf = nc.dram_tensor("identf", [128, 128], F32, kind="ExternalInput").ap()
    d_sel = nc.dram_tensor("sel", [32, 4096], BF16, kind="ExternalInput").ap()
    d_y = nc.dram_tensor("y", [D, T], F32, kind="ExternalOutput").ap()
    d_cc_in = nc.dram_tensor("cc_in", [128, 8192], BF16, kind="Internal").ap()
    d_cc_out = nc.dram_tensor("cc_out", [1024, 8192], BF16, kind="Internal",
                              addr_space="Shared").ap()
    d_dbg = {}
    for (name, shape, dt) in dbg:
        d_dbg[name] = nc.dram_tensor("dbg_" + name, list(shape), dt,
                                     kind="ExternalOutput").ap()

    for s in ['sConst', 'sX', 'sG0', 'sG1', 'sU0', 'sU1', 'sD0', 'sD1', 'sD2',
              'sCI', 'sKV', 'sY', 'sSel', 'sRope', 'sCC']:
        P.new_dma_sem(s)

    with contextlib.ExitStack() as top:
        EE = top.enter_context
        x = [EE(nc.sbuf_tensor(f"x{m}", [128, T], F32)) for m in range(KD)]
        xn = [EE(nc.sbuf_tensor(f"xn{m}", [128, T], BF16)) for m in range(KD)]
        G = [EE(nc.sbuf_tensor(f"Gw{i}", [128, 4096], BF16)) for i in range(2)]
        onesc = EE(nc.sbuf_tensor("onesc", [128, 1], F32))
        onesb = EE(nc.sbuf_tensor("onesb", [128, 1], BF16))
        onesr = EE(nc.sbuf_tensor("onesr", [1, 128], F32))
        identb = EE(nc.sbuf_tensor("identb_s", [128, 128], BF16))
        identf = EE(nc.sbuf_tensor("identf_s", [128, 128], F32))
        qnw = EE(nc.sbuf_tensor("qnw_s", [128, 1], F32))
        qnws = EE(nc.sbuf_tensor("qnw_sc", [128, 1], F32))
        knw = EE(nc.sbuf_tensor("knw_s", [128, 1], F32))
        rtw = EE(nc.sbuf_tensor("rtw_s", [128, 512], F32))

        for (tile, src) in [(onesc, d_onesc), (onesb, d_onesb), (onesr, d_onesr), (identb, d_identb),
                            (identf, d_identf), (qnw, d_qnw),
                            (knw, d_knw), (rtw, d_rtw)]:
            P.dma('S', 'sConst', writes=[tile],
                  emit=lambda eng, t=tile, s=src: eng.dma_start(t[:, :], s[:, :]))
        P.op('V', reads=[qnw], writes=[qnws],
             emit=lambda eng: eng.tensor_scalar(qnws[:, 0:1], qnw[:, 0:1], SCALING, None, ALU.mult))
        for m in range(KD):
            P.dma('S', 'sX', writes=[x[m]],
                  emit=lambda eng, m=m: eng.dma_start(x[m][:, :], d_xT[128 * m:128 * (m + 1), :]))

        def ln_sublayer(sc, with_router):
            sqb, xs32, Ab, Bb, rows = sc['sqb'], sc['xs32'], sc['Ab'], sc['Bb'], sc['rows']
            psS, psB = sc['psS'], sc['psB']
            mean_r, ve_r, rstd_r, t_r = rows
            sum_r, ssq_r, sr_r = mean_r, ve_r, t_r
            for m in range(KD):
                P.op('A', reads=[x[m]], writes=[xn[m]],
                     emit=lambda eng, m=m: eng.copy(xn[m][:, :], x[m][:, :]))
                P.op('A', reads=[xn[m]], writes=[sqb[m % 2]],
                     emit=lambda eng, m=m: eng.activation(sqb[m % 2][:, :], xn[m][:, :], AF.Square))
                for th in range(2):
                    P.op('T', reads=[xn[m], onesb], writes=[psS[th]],
                         emit=lambda eng, m=m, th=th: eng.matmul(
                             psS[th][0:1, :], onesb[:, :], xn[m][:, TH * th:TH * (th + 1)],
                             start=(m == 0), stop=(m == KD - 1)))
                    P.op('T', reads=[sqb[m % 2], onesb], writes=[psS[2 + th]],
                         emit=lambda eng, m=m, th=th: eng.matmul(
                             psS[2 + th][0:1, :], onesb[:, :], sqb[m % 2][:, TH * th:TH * (th + 1)],
                             start=(m == 0), stop=(m == KD - 1)))
            for th in range(2):
                P.op('A', reads=[psS[th]], writes=[sum_r],
                     emit=lambda eng, th=th: eng.copy(sum_r[0:1, TH * th:TH * (th + 1)], psS[th][0:1, :]))
                P.op('A', reads=[psS[2 + th]], writes=[ssq_r],
                     emit=lambda eng, th=th: eng.copy(ssq_r[0:1, TH * th:TH * (th + 1)], psS[2 + th][0:1, :]))
            P.op('V', reads=[sum_r], writes=[mean_r],
                 emit=lambda eng: eng.tensor_scalar(mean_r[0:1, :], sum_r[0:1, :], 1.0 / D, None, ALU.mult))
            P.op('V', reads=[mean_r], writes=[t_r],
                 emit=lambda eng: eng.tensor_tensor(t_r[0:1, :], mean_r[0:1, :], mean_r[0:1, :], ALU.mult))
            P.op('V', reads=[ssq_r], writes=[ve_r],
                 emit=lambda eng: eng.tensor_scalar(ve_r[0:1, :], ssq_r[0:1, :], 1.0 / D, EPS_LN, ALU.mult, ALU.add))
            P.op('V', reads=[ve_r, t_r], writes=[ve_r],
                 emit=lambda eng: eng.tensor_tensor(ve_r[0:1, :], ve_r[0:1, :], t_r[0:1, :], ALU.subtract))
            rsqrt_row(ve_r, rstd_r, t_r)
            P.op('V', reads=[mean_r, rstd_r], writes=[mean_r],
                 emit=lambda eng: eng.tensor_tensor(mean_r[0:1, :], mean_r[0:1, :], rstd_r[0:1, :], ALU.mult))
            for th in range(2):
                P.op('T', reads=[rstd_r, onesr], writes=[psB[0]],
                     emit=lambda eng, th=th: eng.matmul(psB[0][:, :], onesr[0:1, :],
                                                        rstd_r[0:1, TH * th:TH * (th + 1)], start=True, stop=True))
                P.op('A', reads=[psB[0]], writes=[Ab],
                     emit=lambda eng, th=th: eng.copy(Ab[:, TH * th:TH * (th + 1)], psB[0][:, :]))
                P.op('T', reads=[mean_r, onesr], writes=[psB[1]],
                     emit=lambda eng, th=th: eng.matmul(psB[1][:, :], onesr[0:1, :],
                                                        mean_r[0:1, TH * th:TH * (th + 1)], start=True, stop=True))
                P.op('A', reads=[psB[1]], writes=[Bb],
                     emit=lambda eng, th=th: eng.copy(Bb[:, TH * th:TH * (th + 1)], psB[1][:, :]))
            for m in range(KD):
                pp = m % 2
                P.op('V', reads=[x[m], Ab], writes=[xs32[pp]],
                     emit=lambda eng, m=m, pp=pp: eng.tensor_tensor(
                         xs32[pp][:, :], x[m][:, :], Ab[:, :], ALU.mult))
                if with_router:
                    P.op('V', reads=[xs32[pp], Bb], writes=[xs32[pp]],
                         emit=lambda eng, pp=pp: eng.tensor_tensor(
                             xs32[pp][:, :], xs32[pp][:, :], Bb[:, :], ALU.subtract))
                    P.op('A', reads=[xs32[pp]], writes=[xn[m]],
                         emit=lambda eng, m=m, pp=pp: eng.copy(xn[m][:, :], xs32[pp][:, :]))
                    psL = sc['psL']
                    for th in range(2):
                        P.op('T', reads=[xs32[pp], rtw], writes=[psL[th]],
                             emit=lambda eng, m=m, pp=pp, th=th: eng.matmul(
                                 psL[th][0:32, :], rtw[:, 32 * m:32 * (m + 1)],
                                 xs32[pp][:, TH * th:TH * (th + 1)],
                                 start=(m == 0), stop=(m == KD - 1)))
                else:
                    P.op('V', reads=[xs32[pp], Bb], writes=[xn[m]],
                         emit=lambda eng, m=m, pp=pp: eng.tensor_tensor(
                             xn[m][:, :], xs32[pp][:, :], Bb[:, :], ALU.subtract))

        def rsqrt_row(ve_r, rstd_r, t_r, n=T):
            P.op('A', reads=[ve_r], writes=[t_r],
                 emit=lambda eng: eng.activation(t_r[0:1, :n], ve_r[0:1, :n], AF.Ln))
            P.op('A', reads=[t_r], writes=[rstd_r],
                 emit=lambda eng: eng.activation(rstd_r[0:1, :n], t_r[0:1, :n], AF.Exp, scale=-0.5))

        def bc8(tile):
            h = tile[:, 0:8]
            return bass.AP(tensor=h.tensor, offset=h.offset,
                           ap=[[h.ap[0][0], 128], [1, 8], [0, 32]])

        def r3(ap):
            return ap.rearrange("p (a b) -> p a b", a=8)

        def router_topk_part1(sc):
            lg, ltm, etm, ptk, wtm, wfmb = sc['lg'], sc['ltm'], sc['etm'], sc['ptk'], sc['wtm'], sc['wfmb']
            psL, psT = sc['psL'], sc['psB'][0]
            m8, ss8, rs8 = sc['small'][0:3]
            for th in range(2):
                P.op('A', reads=[psL[th]], writes=[lg],
                     emit=lambda eng, th=th: eng.copy(lg[0:32, TH * th:TH * (th + 1)], psL[th][0:32, :]))
            for c in range(8):
                P.op('T', reads=[lg, identf], writes=[psT],
                     emit=lambda eng, c=c: eng.transpose(psT[:, 0:32], lg[0:32, 128 * c:128 * (c + 1)],
                                                         identf[0:32, 0:32]))
                P.op('A', reads=[psT], writes=[ltm],
                     emit=lambda eng, c=c: eng.copy(ltm[:, 32 * c:32 * (c + 1)], psT[:, 0:32]))
            P.op('V', reads=[ltm], writes=[m8],
                 emit=lambda eng: eng.reduce_max(m8[:, 0:8], r3(ltm[:, :]), axis=AX.X))
            P.op('V', reads=[ltm, m8], writes=[ptk],
                 emit=lambda eng: eng.tensor_tensor(r3(ptk[:, :]), r3(ltm[:, :]), bc8(m8), ALU.subtract))
            P.op('A', reads=[ptk], writes=[etm],
                 emit=lambda eng: eng.activation(etm[:, :], ptk[:, :], AF.Exp))
            P.op('A', reads=[etm], writes=[ptk],
                 emit=lambda eng: eng.copy(ptk[:, :], etm[:, :]))
            P.op('V', writes=[ss8],
                 emit=lambda eng: eng.memset(ss8[:, 0:8], 0.0))
            for i in range(8):
                P.op('V', reads=[ptk], writes=[m8],
                     emit=lambda eng: eng.reduce_max(m8[:, 0:8], r3(ptk[:, :]), axis=AX.X))
                P.op('V', reads=[m8, ss8], writes=[ss8],
                     emit=lambda eng: eng.tensor_tensor(ss8[:, 0:8], ss8[:, 0:8], m8[:, 0:8], ALU.add))
                if i < 7:
                    P.op('V', reads=[ptk, m8], writes=[ltm],
                         emit=lambda eng: eng.tensor_tensor(r3(ltm[:, :]), r3(ptk[:, :]), bc8(m8), ALU.is_ge))
                    P.op('V', reads=[ltm, ptk], writes=[wtm],
                         emit=lambda eng: eng.tensor_tensor(wtm[:, :], ltm[:, :], ptk[:, :], ALU.mult))
                    P.op('V', reads=[wtm, ptk], writes=[ptk],
                         emit=lambda eng: eng.tensor_tensor(ptk[:, :], ptk[:, :], wtm[:, :], ALU.subtract))
            P.op('V', reads=[ss8], writes=[rs8],
                 emit=lambda eng: eng.reciprocal(rs8[:, 0:8], ss8[:, 0:8]))
            P.op('V', reads=[rs8], writes=[rs8],
                 emit=lambda eng: eng.tensor_scalar(rs8[:, 0:8], rs8[:, 0:8], 0.5, None, ALU.mult))
            P.op('V', reads=[etm, m8], writes=[ltm],
                 emit=lambda eng: eng.tensor_tensor(r3(ltm[:, :]), r3(etm[:, :]), bc8(m8), ALU.is_ge))
            P.op('V', reads=[ltm, etm], writes=[wtm],
                 emit=lambda eng: eng.tensor_tensor(wtm[:, :], etm[:, :], ltm[:, :], ALU.mult))
            P.op('V', reads=[wtm, rs8], writes=[wtm],
                 emit=lambda eng: eng.tensor_tensor(r3(wtm[:, :]), r3(wtm[:, :]), bc8(rs8), ALU.mult))

        def router_topk_part2(sc):
            wtm, wfmb, psT = sc['wtm'], sc['wfmb'], sc['psB'][0]
            for c in range(8):
                P.op('T', reads=[wtm, identf], writes=[psT],
                     emit=lambda eng, c=c: eng.transpose(psT[0:32, 0:128], wtm[:, 32 * c:32 * (c + 1)],
                                                         identf[:, :]))
                P.op('A', reads=[psT], writes=[wfmb],
                     emit=lambda eng, c=c: eng.copy(wfmb[0:32, 128 * c:128 * (c + 1)], psT[0:32, 0:128]))

        def ffn_pass(pi):
            tag = f'F{pi}_'
            with contextlib.ExitStack() as scx:
                E = scx.enter_context
                U = [E(nc.sbuf_tensor(f"{tag}U{i}", [128, 4096], BF16)) for i in range(2)]
                Dw = [E(nc.sbuf_tensor(f"{tag}D{i}", [128, 1536], BF16)) for i in range(3)]
                h = [E(nc.sbuf_tensor(f"{tag}h{i}", [128, T], BF16)) for i in range(6)]
                wbc = [E(nc.sbuf_tensor(f"{tag}wbc{i}", [128, T], BF16)) for i in range(2)]
                sel_sb = E(nc.sbuf_tensor(f"{tag}sel", [32, 4096], BF16))
                xs32 = [E(nc.sbuf_tensor(f"{tag}xs{i}", [128, T], F32)) for i in range(2)]
                Ab = E(nc.sbuf_tensor(f"{tag}Ab", [128, T], F32))
                Bb = E(nc.sbuf_tensor(f"{tag}Bb", [128, T], F32))
                rows = [E(nc.sbuf_tensor(f"{tag}row{i}", [1, T], F32)) for i in range(4)]
                lg = E(nc.sbuf_tensor(f"{tag}lg", [32, T], F32))
                ltm = E(nc.sbuf_tensor(f"{tag}ltm", [128, 256], F32))
                etm = E(nc.sbuf_tensor(f"{tag}etm", [128, 256], F32))
                ptk = E(nc.sbuf_tensor(f"{tag}ptk", [128, 256], F32))
                wtm = E(nc.sbuf_tensor(f"{tag}wtm", [128, 256], F32))
                wfmb = E(nc.sbuf_tensor(f"{tag}wfmb", [32, T], BF16))
                small = [E(nc.sbuf_tensor(f"{tag}sm{i}", [128, 32], F32)) for i in range(3)]
                psG = [E(nc.psum_tensor(f"{tag}psG{i}", [128, 512], F32)) for i in range(4)]
                psU = [E(nc.psum_tensor(f"{tag}psU{i}", [128, 512], F32)) for i in range(4)]
                sc = dict(sqb=h[0:2], xs32=xs32, Ab=Ab, Bb=Bb, rows=rows, lg=lg, ltm=ltm,
                          etm=etm, ptk=ptk, wtm=wtm, wfmb=wfmb, small=small,
                          psS=psG, psB=psU[:2], psL=psU[2:4])

                P.dma('S', 'sSel', writes=[sel_sb],
                      emit=lambda eng: eng.dma_start(sel_sb[:, :], d_sel[:, :]))

                loadq = []

                def push_load(buf, sem, lo, hi):
                    def planit():
                        P.dma('P', sem, writes=[buf],
                              emit=lambda eng: eng.dma_start(buf[:, 0:hi - lo], d_wffn[:, lo:hi]))
                    loadq.append(planit)

                for e in range(n_exp):
                    base = 36864 * e
                    for g in range(3):
                        sl = (3 * e + g) % 2
                        push_load(G[sl], f'sG{sl}', base + 4096 * g, base + 4096 * (g + 1))
                        push_load(U[sl], f'sU{sl}', base + 12288 + 4096 * g, base + 12288 + 4096 * (g + 1))
                    for q in range(8):
                        sl = (8 * e + q) % 3
                        push_load(Dw[sl], f'sD{sl}', base + 24576 + 1536 * q, base + 24576 + 1536 * (q + 1))
                pos = [0]

                def pump(n):
                    n = min(n, len(loadq))
                    while pos[0] < n:
                        loadq[pos[0]]()
                        pos[0] += 1

                pump(4)
                ln_sublayer(sc, True)
                router_topk_part1(sc)

                def bank1(e):
                    return psG if e % 2 == 0 else psU

                def bank2(e):
                    return psU if e % 2 == 0 else psG

                def wbc_mm(e):
                    b = e % 2
                    pt = bank1(e)
                    for th in range(2):
                        P.op('T', reads=[wfmb, sel_sb], writes=[pt[th]],
                             emit=lambda eng, e=e, th=th, pt=pt: eng.matmul(
                                 pt[th][:, :], sel_sb[0:32, 128 * e:128 * (e + 1)],
                                 wfmb[0:32, TH * th:TH * (th + 1)], start=True, stop=True))
                        P.op('A', reads=[pt[th]], writes=[wbc[b]],
                             emit=lambda eng, b=b, th=th, pt=pt: eng.copy(wbc[b][:, TH * th:TH * (th + 1)], pt[th][:, :]))

                for e in range(n_exp):
                    ldbase = 14 * e
                    P1, P2 = bank1(e), bank2(e)
                    for g in range(3):
                        pump(ldbase + 4 + 2 * g)
                        sl = (3 * e + g) % 2

                        def em_gate(eng, sl=sl, P1=P1):
                            for k in range(16):
                                for mh in range(2):
                                    for th in range(2):
                                        mm = eng.matmul(P1[2 * mh + th][:, :],
                                                        G[sl][:, 256 * k + 128 * mh:256 * k + 128 * (mh + 1)],
                                                        xn[k][:, TH * th:TH * (th + 1)],
                                                        start=(k == 0), stop=(k == 15))
                            return mm
                        P.op('T', reads=[G[sl]] + xn, writes=P1, emit=em_gate)

                        def em_up(eng, sl=sl, P2=P2):
                            for k in range(16):
                                for mh in range(2):
                                    for th in range(2):
                                        mm = eng.matmul(P2[2 * mh + th][:, :],
                                                        U[sl][:, 256 * k + 128 * mh:256 * k + 128 * (mh + 1)],
                                                        xn[k][:, TH * th:TH * (th + 1)],
                                                        start=(k == 0), stop=(k == 15))
                            return mm
                        P.op('T', reads=[U[sl]] + xn, writes=P2, emit=em_up)
                        for mh in range(2):
                            hm = 2 * g + mh
                            for th in range(2):
                                P.op('A', reads=[P1[2 * mh + th]], writes=[h[hm]],
                                     emit=lambda eng, hm=hm, mh=mh, th=th, P1=P1: eng.activation(
                                         h[hm][:, TH * th:TH * (th + 1)], P1[2 * mh + th][:, :], AF.Silu))
                                P.op('V', reads=[P2[2 * mh + th], h[hm]], writes=[h[hm]],
                                     emit=lambda eng, hm=hm, mh=mh, th=th, P2=P2: eng.tensor_tensor(
                                         h[hm][:, TH * th:TH * (th + 1)], P2[2 * mh + th][:, :],
                                         h[hm][:, TH * th:TH * (th + 1)], ALU.mult))
                            if e > 0:
                                P.op('V', reads=[h[hm], wbc[e % 2]], writes=[h[hm]],
                                     emit=lambda eng, hm=hm, e=e: eng.tensor_tensor(
                                         h[hm][:, :], h[hm][:, :], wbc[e % 2][:, :], ALU.mult))
                    if e == 0:
                        router_topk_part2(sc)
                        wbc_mm(0)
                        for hm in range(6):
                            P.op('V', reads=[h[hm], wbc[0]], writes=[h[hm]],
                                 emit=lambda eng, hm=hm: eng.tensor_tensor(
                                     h[hm][:, :], h[hm][:, :], wbc[0][:, :], ALU.mult))
                    if e + 1 < n_exp:
                        wbc_mm(e + 1)
                    for m2 in range(16):
                        pump(ldbase + 9 + m2 // 2)
                        sl = (8 * e + m2 // 2) % 3

                        def em_down(eng, m2=m2, sl=sl, P1=P1):
                            for k in range(6):
                                for th in range(2):
                                    mm = eng.matmul(P1[2 * (m2 % 2) + th][:, :],
                                                    Dw[sl][:, 768 * (m2 % 2) + 128 * k:768 * (m2 % 2) + 128 * (k + 1)],
                                                    h[k][:, TH * th:TH * (th + 1)],
                                                    start=(k == 0), stop=(k == 5))
                            return mm
                        P.op('T', reads=[Dw[sl]] + h, writes=[P1[2 * (m2 % 2)], P1[2 * (m2 % 2) + 1]],
                             emit=em_down)
                        for th in range(2):
                            P.op('V', reads=[P1[2 * (m2 % 2) + th], x[m2]], writes=[x[m2]],
                                 emit=lambda eng, m2=m2, th=th, P1=P1: eng.tensor_tensor(
                                     x[m2][:, TH * th:TH * (th + 1)], P1[2 * (m2 % 2) + th][:, :],
                                     x[m2][:, TH * th:TH * (th + 1)], ALU.add))
                for name in list(d_dbg):
                    if name.startswith(f'ffn{pi}'):
                        dump_dbg(name)
                P.barrier()

        def dump_dbg(name):
            tgt = name.split('_', 1)[1]
            src = {'x': x, 'xn': xn}[tgt]
            for m in range(KD):
                P.dma('S', 'sY', reads=[src[m]],
                      emit=lambda eng, m=m, name=name: eng.dma_start(
                          d_dbg[name][128 * m:128 * (m + 1), :], src[m][:, :]))

        def attn_pass():
            tag = 'AT_'
            with contextlib.ExitStack() as lsc:
                E = lsc.enter_context
                xs2 = [E(nc.sbuf_tensor(f"{tag}lxs{i}", [128, T], F32)) for i in range(2)]
                sq2 = [E(nc.sbuf_tensor(f"{tag}lsq{i}", [128, T], BF16)) for i in range(2)]
                Ab2 = E(nc.sbuf_tensor(f"{tag}lAb", [128, T], F32))
                Bb2 = E(nc.sbuf_tensor(f"{tag}lBb", [128, T], F32))
                lrows = [E(nc.sbuf_tensor(f"{tag}lrow{i}", [1, T], F32)) for i in range(4)]
                psS = [E(nc.psum_tensor(f"{tag}lpsS{i}", [128, 512], F32)) for i in range(4)]
                psB = [E(nc.psum_tensor(f"{tag}lpsB{i}", [128, 512], F32)) for i in range(2)]
                ln_sublayer(dict(sqb=sq2, xs32=xs2, Ab=Ab2, Bb=Bb2, rows=lrows,
                                 psS=psS, psB=psB, psL=None), False)
            P.barrier()

            with contextlib.ExitStack() as scx:
                E = scx.enter_context
                cosT = E(nc.sbuf_tensor(f"{tag}cos", [128, T], F32))
                sinT = E(nc.sbuf_tensor(f"{tag}sin", [128, T], F32))
                sqA = E(nc.sbuf_tensor(f"{tag}sq", [128, T], F32))
                rb_sb = E(nc.sbuf_tensor(f"{tag}rb", [128, TH], F32))
                rows = [E(nc.sbuf_tensor(f"{tag}row{i}", [1, T], F32)) for i in range(3)]
                tri_s = E(nc.sbuf_tensor(f"{tag}tri", [128, 2048], BF16))
                kb = E(nc.sbuf_tensor(f"{tag}kb", [128, T], BF16))
                vb = E(nc.sbuf_tensor(f"{tag}vb", [128, T], BF16))
                vt = E(nc.sbuf_tensor(f"{tag}vt", [128, T], BF16))
                K = [E(nc.sbuf_tensor(f"{tag}K{g}", [128, 2048], BF16)) for g in range(4)]
                V = [E(nc.sbuf_tensor(f"{tag}V{g}", [128, 2048], BF16)) for g in range(4)]
                QO = [E(nc.sbuf_tensor(f"{tag}QO{i}", [128, T], BF16)) for i in range(8)]
                e_sb = E(nc.sbuf_tensor(f"{tag}esb", [128, 2048], BF16))
                eTq = E(nc.sbuf_tensor(f"{tag}eTq", [128, 2048], BF16))
                ssum = E(nc.sbuf_tensor(f"{tag}ssum", [128, 16], F32))
                rsc = E(nc.sbuf_tensor(f"{tag}rsc", [128, 8], F32))
                nmx = E(nc.sbuf_tensor(f"{tag}nmx", [128, 16], F32))
                bia = E(nc.sbuf_tensor(f"{tag}bia", [128, 8], F32))
                pscore = E(nc.psum_tensor(f"{tag}pscore", [128, 2048], F32))
                ptp = E(nc.psum_tensor(f"{tag}ptp", [128, 1024], BF16))
                psO = [E(nc.psum_tensor(f"{tag}psO{i}", [128, 512], F32)) for i in range(2)]
                psx = E(nc.psum_tensor(f"{tag}psx", [128, 512], F32))
                ve_r, rstd_r, t_r = rows

                P.dma('S', 'sRope', writes=[cosT],
                      emit=lambda eng: eng.dma_start(cosT[:, :], d_cos[:, :]))
                P.dma('S', 'sRope', writes=[sinT],
                      emit=lambda eng: eng.dma_start(sinT[:, :], d_sin[:, :]))
                P.dma('S', 'sRope', writes=[tri_s],
                      emit=lambda eng: eng.dma_start(tri_s[:, :], d_tri[:, :]))

                loadq = []

                def push_wload(lo, hi, src=d_wqkv):
                    i = len(loadq)
                    sl = i % 2

                    def planit(sl=sl, lo=lo, hi=hi, src=src):
                        P.dma('P', f'sG{sl}', writes=[G[sl]],
                              emit=lambda eng: eng.dma_start(G[sl][:, 0:hi - lo], src[:, lo:hi]))
                    loadq.append(planit)
                    return i, sl

                ld_ids = {}
                for hk in range(4):
                    ld_ids[('k', hk)] = push_wload(2048 * hk, 2048 * (hk + 1))
                for hk in range(4):
                    ld_ids[('v', hk)] = push_wload(8192 + 2048 * hk, 8192 + 2048 * (hk + 1))
                for w in range(n_heads // 8):
                    for i in range(8):
                        hq = 8 * w + i
                        ld_ids[('q', hq)] = push_wload(16384 + 2048 * hq, 16384 + 2048 * (hq + 1))
                    for j4 in range(4):
                        ld_ids[('o', w, j4)] = push_wload(16384 * w + 4096 * j4,
                                                          16384 * w + 4096 * (j4 + 1), src=d_wo)
                pos = [0]

                def pump(n):
                    n = min(n, len(loadq))
                    while pos[0] < n:
                        loadq[pos[0]]()
                        pos[0] += 1

                PK = [f'{tag}pk{i}' for i in range(4)]

                def qps(par, th):
                    if par == 0:
                        return psO[th][:, :], rk(psO[th])
                    return pscore[:, 1024 + TH * th:1024 + TH * (th + 1)], PK[2 + th]

                def head_mm(sl, par):
                    aps = [qps(par, th) for th in range(2)]

                    def em(eng, sl=sl, aps=aps):
                        for k in range(16):
                            for th in range(2):
                                mm = eng.matmul(aps[th][0], G[sl][:, 128 * k:128 * (k + 1)],
                                                xn[k][:, TH * th:TH * (th + 1)],
                                                start=(k == 0), stop=(k == 15))
                        return mm
                    P.op('T', reads=[G[sl]] + xn, writes=[a[1] for a in aps], emit=em)

                def qokeys(dst):
                    nm = rk(dst)
                    if nm.startswith(f'{tag}QO'):
                        i = nm[len(f'{tag}QO'):]
                        return [f'QO{i}_lo', f'QO{i}_hi']
                    return [nm]

                def rms_rope(norm_w, dst, par):
                    aps = [qps(par, th) for th in range(2)]
                    for th in range(2):
                        P.op('A', reads=[aps[th][1]], writes=[vb],
                             emit=lambda eng, th=th, aps=aps: eng.activation(
                                 vb[:, TH * th:TH * (th + 1)], aps[th][0], AF.Square))
                        P.op('P', reads=[vb], writes=[t_r],
                             emit=lambda eng, th=th: eng.tensor_reduce(
                                 t_r[0:1, TH * th:TH * (th + 1)], vb[:, TH * th:TH * (th + 1)],
                                 AX.C, ALU.add))
                    P.op('V', reads=[t_r], writes=[ve_r],
                         emit=lambda eng: eng.tensor_scalar(ve_r[0:1, :], t_r[0:1, 0:T],
                                                            1.0 / HD, EPS_RMS, ALU.mult, ALU.add))
                    rsqrt_row(ve_r, rstd_r, t_r)
                    for th in range(2):
                        P.op('T', reads=[rstd_r, onesr], writes=[psx],
                             emit=lambda eng, th=th: eng.matmul(psx[:, :], onesr[0:1, :],
                                                                rstd_r[0:1, TH * th:TH * (th + 1)],
                                                                start=True, stop=True))
                        P.op('A', reads=[psx], writes=[rb_sb],
                             emit=lambda eng, th=th: eng.copy(rb_sb[:, :], psx[:, :]))
                        P.op('V', reads=[aps[th][1], norm_w, rb_sb], writes=[PK[th]],
                             emit=lambda eng, th=th, aps=aps: eng.scalar_tensor_tensor(
                                 pscore[:, TH * th:TH * (th + 1)], aps[th][0], norm_w[:, 0:1],
                                 rb_sb[:, :], ALU.mult, ALU.mult))
                    so = slice(0, T)
                    kq = pscore
                    dk = qokeys(dst)
                    P.op('V', reads=[PK[0], PK[1], cosT], writes=[sqA],
                         emit=lambda eng: eng.tensor_tensor(sqA[0:64, :], kq[0:64, 0:T], cosT[0:64, :], ALU.mult))
                    P.op('V', reads=[PK[0], PK[1], sinT], writes=dk,
                         emit=lambda eng: eng.scalar_tensor_tensor(
                             dst[0:64, so], kq[64:128, 0:T], -1.0, sinT[0:64, :], ALU.mult, ALU.mult))
                    P.op('P', reads=dk + [sqA], writes=dk,
                         emit=lambda eng: eng.tensor_tensor(dst[0:64, so], dst[0:64, so], sqA[0:64, :], ALU.add))
                    P.op('V', reads=[PK[0], PK[1], cosT], writes=[sqA],
                         emit=lambda eng: eng.tensor_tensor(sqA[64:128, :], kq[64:128, 0:T], cosT[64:128, :], ALU.mult))
                    P.op('V', reads=[PK[0], PK[1], sinT], writes=dk,
                         emit=lambda eng: eng.tensor_tensor(
                             dst[64:128, so], kq[0:64, 0:T], sinT[64:128, :], ALU.mult))
                    P.op('P', reads=dk + [sqA], writes=dk,
                         emit=lambda eng: eng.tensor_tensor(dst[64:128, so], dst[64:128, so], sqA[64:128, :], ALU.add))

                ep = [None]

                def flush_ep():
                    if ep[0]:
                        ep[0]()
                        ep[0] = None

                for hk in range(4):
                    i, sl = ld_ids[('k', hk)]
                    pump(i + 2)
                    head_mm(sl, hk % 2)
                    flush_ep()

                    def mk(hk=hk):
                        def f():
                            rms_rope(knw, kb, hk % 2)
                            P.dma('S', 'sCI', reads=[kb], writes=['cc_in'],
                                  emit=lambda eng, hk=hk: eng.dma_start(
                                      d_cc_in[:, 1024 * hk:1024 * (hk + 1)], kb[:, :]))
                        return f
                    ep[0] = mk()
                for hk in range(4):
                    i, sl = ld_ids[('v', hk)]
                    pump(i + 2)
                    head_mm(sl, hk % 2)
                    flush_ep()

                    def mkv(hk=hk):
                        def f():
                            aps = [qps(hk % 2, th) for th in range(2)]
                            for th in range(2):
                                P.op('A', reads=[aps[th][1]], writes=[vb],
                                     emit=lambda eng, th=th, aps=aps: eng.copy(vb[:, TH * th:TH * (th + 1)], aps[th][0]))

                            def em_vt(eng):
                                for i2 in range(8):
                                    mm = eng.transpose(ptp[:, 128 * i2:128 * (i2 + 1)],
                                                       vb[:, 128 * i2:128 * (i2 + 1)], identb[:, :])
                                return mm
                            P.op('T', reads=[vb, identb], writes=[ptp], emit=em_vt)
                            P.op('V', reads=[ptp], writes=[vt],
                                 emit=lambda eng: eng.tensor_copy(vt[:, :], ptp[:, :]))
                            P.dma('S', 'sCI', reads=[vt], writes=['cc_in'],
                                  emit=lambda eng, hk=hk: eng.dma_start(
                                      d_cc_in[:, 4096 + 1024 * hk:4096 + 1024 * (hk + 1)], vt[:, :]))
                        return f
                    ep[0] = mkv()
                flush_ep()

                P.op('P', reads=['cc_in'], writes=['cc_out'], inc=1,
                     emit=lambda eng: eng.collective_compute(
                         "AllGather", ALU.bypass,
                         replica_groups=[[0, 1, 2, 3, 4, 5, 6, 7]],
                         ins=[d_cc_in[:, :]], outs=[d_cc_out[:, :]]))

                def rb_body(pair):
                    r0 = 256 * pair
                    r1 = 256 * pair + 128
                    for g in range(4):
                        for (dstc, rr, sc0) in [(0, r0, 0), (512, r1, 0), (1024, r1, 512), (1536, r0, 512)]:
                            P.dma('A', 'sKV', reads=['cc_out'], writes=[K[g]],
                                  emit=lambda eng, g=g, dstc=dstc, rr=rr, sc0=sc0: eng.dma_start(
                                      K[g][:, dstc:dstc + 512],
                                      d_cc_out[rr:rr + 128, 1024 * g + sc0:1024 * g + sc0 + 512]))
                        for (dstc, rr, sc0) in [(0, r0, 0), (512, r1, 0), (1024, r1, 512), (1536, r0, 512)]:
                            P.dma('A', 'sKV', reads=['cc_out'], writes=[V[g]],
                                  emit=lambda eng, g=g, dstc=dstc, rr=rr, sc0=sc0: eng.dma_start(
                                      V[g][:, dstc:dstc + 512],
                                      d_cc_out[rr:rr + 128, 4096 + 1024 * g + sc0:4096 + 1024 * g + sc0 + 512]))
                P.switch(['A'], lambda pid: pid // 2, 4, rb_body)

                def attn_head(par, hq, slot, npend):
                    g = hq // 8
                    blocks = ([(0, 512, 0), (512, 2048, 1536)] if par == 0
                              else [(0, 1024, 512), (512, 1536, 1024)])
                    for b_idx, (qoff, ktl, dbase) in enumerate(blocks):
                        qk = f'QO{slot}_lo' if b_idx == 0 else f'QO{slot}_hi'
                        npieces = 1 if ktl <= 1024 else 2
                        pend = [None]
                        units = [(qc, p) for qc in range(4) for p in range(npieces)]
                        for ui, (qc, p) in enumerate(units):
                            plen = min(ktl, 1024 * (p + 1)) - 1024 * p
                            roff = (1024 * (qc % 2)) if npieces == 1 else 1024 * p
                            kys = PK[roff // 512:(roff + plen + 511) // 512]
                            acc = psO[b_idx] if p == 0 else psx
                            acck = rk(acc)

                            def em_sc(eng, qoff=qoff, qc=qc, plen=plen, g=g, slot=slot, roff=roff, p=p):
                                for kc5 in range(plen // 512):
                                    mm = eng.matmul(pscore[:, roff + 512 * kc5:roff + 512 * (kc5 + 1)],
                                                    QO[slot][:, qoff + 128 * qc:qoff + 128 * (qc + 1)],
                                                    K[g][:, 1024 * p + 512 * kc5:1024 * p + 512 * (kc5 + 1)],
                                                    start=True, stop=True)
                                return mm
                            P.op('T', reads=[qk, K[g]], writes=kys, emit=em_sc)
                            if ui == 1 and npend[0]:
                                npend[0]()
                                npend[0] = None
                            if pend[0]:
                                pend[0]()
                            if npieces == 1 or dbase // 1024 == p:
                                db = dbase if npieces == 1 else dbase - 1024 * p
                                P.op('V', reads=kys + [tri_s], writes=kys,
                                     emit=lambda eng, db=db, qc=qc, roff=roff: eng.tensor_tensor(
                                         pscore[:, roff + db:roff + db + 512],
                                         pscore[:, roff + db:roff + db + 512],
                                         tri_s[:, 512 * qc:512 * (qc + 1)], ALU.add))
                            nmcol = (2 * qc + p) if npieces == 2 else 8 + qc
                            P.op('V', reads=kys, writes=[nmx],
                                 emit=lambda eng, plen=plen, roff=roff, nmcol=nmcol: eng.reduce_max(
                                     nmx[:, nmcol:nmcol + 1], pscore[:, roff:roff + plen],
                                     axis=AX.X, negate=True))
                            ek = 'esb_lo' if roff == 0 else 'esb_hi'
                            scol = (qc + 4 * p) if npieces == 2 else 8 + qc
                            P.op('A', reads=kys + [nmx], writes=[ek, ssum],
                                 emit=lambda eng, plen=plen, roff=roff, nmcol=nmcol, scol=scol: eng.activation(
                                     e_sb[:, roff:roff + plen], pscore[:, roff:roff + plen], AF.Exp,
                                     bias=nmx[:, nmcol:nmcol + 1], scale=1.0,
                                     accum_out=ssum[:, scol:scol + 1]))

                            def mk_pend(qc=qc, p=p, plen=plen, roff=roff, g=g, acc=acc, acck=acck, ek=ek,
                                        npieces=npieces, ktl=ktl):
                                def pendf():
                                    nkcp = plen // 128
                                    base = roff if npieces == 1 else 1024 * p

                                    def em_tp(eng):
                                        for i2 in range(nkcp):
                                            mm = eng.transpose(ptp[:, 128 * i2:128 * (i2 + 1)],
                                                               e_sb[:, roff + 128 * i2:roff + 128 * (i2 + 1)],
                                                               identb[:, :])
                                        return mm
                                    P.op('T', reads=[ek, identb], writes=[ptp], emit=em_tp)
                                    P.op('V', reads=[ptp], writes=[eTq],
                                         emit=lambda eng: eng.tensor_copy(
                                             eTq[:, 0:128 * nkcp], ptp[:, 0:128 * nkcp]))

                                    def em_av(eng):
                                        for i2 in range(nkcp):
                                            mm = eng.matmul(acc[:, 128 * qc:128 * (qc + 1)],
                                                            V[g][:, 1024 * p + 128 * i2:1024 * p + 128 * (i2 + 1)],
                                                            eTq[:, 128 * i2:128 * (i2 + 1)],
                                                            start=(i2 == 0), stop=(i2 == nkcp - 1))
                                        return mm
                                    P.op('T', reads=[eTq, V[g]], writes=[acck], emit=em_av)
                                return pendf
                            pend[0] = mk_pend()
                        pend[0]()

                        def mk_norm(b_idx=b_idx, qoff=qoff, slot=slot, qk=qk, npieces=npieces):
                            def normf():
                                if npieces == 1:
                                    P.op('V', reads=[ssum], writes=[rsc],
                                         emit=lambda eng: eng.reciprocal(rsc[:, 0:4], ssum[:, 8:12]))

                                    def em_rst(eng):
                                        for qc in range(4):
                                            mm = eng.transpose(pscore[0:1, 128 * qc:128 * (qc + 1)],
                                                               rsc[:, qc:qc + 1], identf[:, :])
                                        return mm
                                    P.op('T', reads=[rsc, identf], writes=[PK[0]], emit=em_rst)
                                    P.op('A', reads=[PK[0]], writes=[t_r],
                                         emit=lambda eng: eng.copy(t_r[0:1, 0:512], pscore[0:1, 0:512]))
                                    P.op('T', reads=[t_r, onesr], writes=[PK[0]],
                                         emit=lambda eng: eng.matmul(pscore[:, 0:512], onesr[0:1, :],
                                                                     t_r[0:1, 0:512], start=True, stop=True))
                                    P.op('A', reads=[PK[0]], writes=[rb_sb],
                                         emit=lambda eng: eng.copy(rb_sb[:, :], pscore[:, 0:512]))
                                    P.op('V', reads=[rk(psO[b_idx]), rb_sb], writes=[qk],
                                         emit=lambda eng: eng.tensor_tensor(
                                             QO[slot][:, qoff:qoff + 512], psO[b_idx][:, :], rb_sb[:, :], ALU.mult))
                                else:
                                    nm0 = bass.AP(tensor=nmx[:, 0:8].tensor, offset=nmx[:, 0:8].offset,
                                                  ap=[[nmx[:, 0:8].ap[0][0], 128], [2, 4]])
                                    nm1 = bass.AP(tensor=nmx[:, 0:8].tensor, offset=nmx[:, 0:8].offset + 1,
                                                  ap=[[nmx[:, 0:8].ap[0][0], 128], [2, 4]])
                                    P.op('V', reads=[nmx], writes=[bia],
                                         emit=lambda eng: eng.tensor_tensor(bia[:, 0:4], nm0, nm1, ALU.min))
                                    P.op('V', reads=[bia, nmx], writes=[rsc],
                                         emit=lambda eng: eng.tensor_tensor(rsc[:, 0:4], bia[:, 0:4], nm0, ALU.subtract))
                                    P.op('V', reads=[bia, nmx], writes=[rsc],
                                         emit=lambda eng: eng.tensor_tensor(rsc[:, 4:8], bia[:, 0:4], nm1, ALU.subtract))
                                    P.op('A', reads=[rsc], writes=[rsc],
                                         emit=lambda eng: eng.activation(rsc[:, 0:8], rsc[:, 0:8], AF.Exp))
                                    P.op('V', reads=[ssum, rsc], writes=[bia],
                                         emit=lambda eng: eng.tensor_tensor(bia[:, 4:8], ssum[:, 0:4], rsc[:, 0:4], ALU.mult))
                                    P.op('V', reads=[ssum, rsc, bia], writes=[ssum],
                                         emit=lambda eng: eng.tensor_tensor(ssum[:, 4:8], ssum[:, 4:8], rsc[:, 4:8], ALU.mult))
                                    P.op('V', reads=[bia, ssum], writes=[bia],
                                         emit=lambda eng: eng.tensor_tensor(bia[:, 4:8], bia[:, 4:8], ssum[:, 4:8], ALU.add))
                                    P.op('V', reads=[bia], writes=[bia],
                                         emit=lambda eng: eng.reciprocal(bia[:, 0:4], bia[:, 4:8]))
                                    P.op('V', reads=[rsc, bia], writes=[rsc],
                                         emit=lambda eng: eng.tensor_tensor(rsc[:, 0:4], rsc[:, 0:4], bia[:, 0:4], ALU.mult))
                                    P.op('V', reads=[rsc, bia], writes=[rsc],
                                         emit=lambda eng: eng.tensor_tensor(rsc[:, 4:8], rsc[:, 4:8], bia[:, 0:4], ALU.mult))

                                    def em_rst(eng):
                                        for j in range(8):
                                            mm = eng.transpose(pscore[0:1, 128 * j:128 * (j + 1)],
                                                               rsc[:, j:j + 1], identf[:, :])
                                        return mm
                                    P.op('T', reads=[rsc, identf], writes=[PK[0], PK[1]], emit=em_rst)
                                    P.op('A', reads=[PK[0], PK[1]], writes=[t_r],
                                         emit=lambda eng: eng.copy(t_r[0:1, 0:1024], pscore[0:1, 0:1024]))
                                    P.op('T', reads=[t_r, onesr], writes=[PK[0]],
                                         emit=lambda eng: eng.matmul(pscore[:, 0:512], onesr[0:1, :],
                                                                     t_r[0:1, 0:512], start=True, stop=True))
                                    P.op('A', reads=[PK[0]], writes=[rb_sb],
                                         emit=lambda eng: eng.copy(rb_sb[:, :], pscore[:, 0:512]))
                                    P.op('T', reads=[t_r, onesr], writes=[PK[1]],
                                         emit=lambda eng: eng.matmul(pscore[:, 512:1024], onesr[0:1, :],
                                                                     t_r[0:1, 512:1024], start=True, stop=True))
                                    P.op('A', reads=[PK[1]], writes=[sqA],
                                         emit=lambda eng: eng.copy(sqA[:, 0:512], pscore[:, 512:1024]))
                                    P.op('V', reads=[rk(psO[b_idx]), rb_sb], writes=[qk],
                                         emit=lambda eng: eng.tensor_tensor(
                                             QO[slot][:, qoff:qoff + 512], psO[b_idx][:, :], rb_sb[:, :], ALU.mult))
                                    P.op('V', reads=[rk(psx), sqA], writes=[rb_sb],
                                         emit=lambda eng: eng.tensor_tensor(
                                             rb_sb[:, :], psx[:, :], sqA[:, 0:512], ALU.mult))
                                    P.op('V', reads=[rb_sb, qk], writes=[qk],
                                         emit=lambda eng: eng.tensor_tensor(
                                             QO[slot][:, qoff:qoff + 512], QO[slot][:, qoff:qoff + 512],
                                             rb_sb[:, :], ALU.add))
                            return normf
                        if npend[0]:
                            npend[0]()
                        npend[0] = mk_norm()

                for w in range(n_heads // 8):
                    for i in range(8):
                        hq = 8 * w + i
                        li, sl = ld_ids[('q', hq)]
                        pump(li + 2)
                        head_mm(sl, i % 2)
                        flush_ep()

                        def mkq(i=i):
                            return lambda: rms_rope(qnws, QO[i], i % 2)
                        ep[0] = mkq()
                    flush_ep()

                    def wave_attn(par, w=w):
                        npend = [None]
                        for i in range(8):
                            attn_head(par, 8 * w + i, i, npend)
                        if npend[0]:
                            npend[0]()
                    P.switch(['T', 'V', 'A'], lambda pid: pid % 2, 2, wave_attn)

                    for m2 in range(16):
                        li, sl = ld_ids[('o', w, m2 // 4)]
                        pump(li + 2)
                        aps = [qps(m2 % 2, th) for th in range(2)]

                        def em_op(eng, sl=sl, m2=m2, aps=aps):
                            for kk in range(8):
                                for th in range(2):
                                    mm = eng.matmul(aps[th][0],
                                                    G[sl][:, 1024 * (m2 % 4) + 128 * kk:1024 * (m2 % 4) + 128 * (kk + 1)],
                                                    QO[kk][:, TH * th:TH * (th + 1)],
                                                    start=(kk == 0), stop=(kk == 7))
                            return mm
                        P.op('T', reads=[G[sl]] + [k for kk in range(8) for k in (f'QO{kk}_lo', f'QO{kk}_hi')],
                             writes=[a[1] for a in aps], emit=em_op)
                        for th in range(2):
                            P.op('V', reads=[aps[th][1], x[m2]], writes=[x[m2]],
                                 emit=lambda eng, m2=m2, th=th, aps=aps: eng.tensor_tensor(
                                     x[m2][:, TH * th:TH * (th + 1)], aps[th][0],
                                     x[m2][:, TH * th:TH * (th + 1)], ALU.add))
                P.barrier()


        if 'ffn0' in phases:
            ffn_pass(0)
        if 'attn' in phases:
            attn_pass()
        if 'ffn1' in phases:
            ffn_pass(1)

        for m in range(KD):
            P.dma('S', 'sY', reads=[x[m]],
                  emit=lambda eng, m=m: eng.dma_start(d_y[128 * m:128 * (m + 1), :], x[m][:, :]))
        P.raw('S', lambda eng: eng.wait_ge(P._sem_handles['sY'], P.dma_sems['sY']))

        with contextlib.ExitStack() as sctx:
            SE = sctx.enter_context
            for s in ['T', 'V', 'A', 'P', 'S']:
                P._sem_handles[s] = SE(nc.semaphore(f"sem{s}"))
            for s in P.dma_sems:
                P._sem_handles[s] = SE(nc.semaphore(f"sem_{s}"))
            import os
            if os.environ.get('DUMP_SEMS'):
                for k, v in P._sem_handles.items():
                    print('SEMMAP', k, v)
            block = SE(nc.Block())

            @block.tensor
            def _(eng):
                P.run_engine('T', eng, nc)

            @block.vector
            def _(eng):
                P.run_engine('V', eng, nc)

            @block.scalar
            def _(eng):
                P.run_engine('A', eng, nc)

            @block.gpsimd
            def _(eng):
                P.run_engine('P', eng, nc)

            @block.sync
            def _(eng):
                P.run_engine('S', eng, nc)

    return nc


def kernel(**inputs):
    per_core = prep_host(inputs)
    nc = build_program()
    res = run_bass_kernel_spmd(nc, per_core, core_ids=list(range(NCORES)))
    return reassemble(res.results)


# version 12
# speedup vs baseline: 1.4143x; 1.4143x over previous
import contextlib
import numpy as np
import ml_dtypes

import concourse.bass as bass
import concourse.mybir as mybir
from concourse.bass_utils import run_bass_kernel_spmd

F32 = mybir.dt.float32
BF16 = mybir.dt.bfloat16
AF = mybir.ActivationFunctionType
ALU = mybir.AluOpType
AX = mybir.AxisListType
ET = mybir.EngineType

NCORES = 8
D = 2048
T = 1024
TH = 512
KD = 16
NEXP = 32
NH, NKV, HD = 32, 4, 128
SCALING = HD ** -0.5
EPS_LN = 1e-5
EPS_RMS = 1e-6
NEG = -1e30

ENGMAP = {'T': ET.PE, 'V': ET.DVE, 'A': ET.Activation, 'P': ET.Pool, 'S': ET.SP}


def rk(t):
    return t if isinstance(t, str) else t.name


class Plan:

    def __init__(self):
        self.cl = {e: [] for e in ENGMAP}
        self.eng_sem = {e: 0 for e in ENGMAP}
        self.dma_sems = {}
        self.lastw = {}
        self.readers = {}
        self.wm = {e: {} for e in ENGMAP}
        self._sem_handles = {}

    def new_dma_sem(self, name):
        self.dma_sems[name] = 0

    def _emit_wait(self, e, sem_name, val):
        if val <= 0 or self.wm[e].get(sem_name, 0) >= val:
            return
        self.wm[e][sem_name] = val

        def f(eng, plan=self, sem_name=sem_name, val=val):
            eng.wait_ge(plan._sem_handles[sem_name], val)
        self.cl[e].append(f)

    def _dep_waits(self, e, reads, writes):
        need = {}
        for r in reads:
            for sname, c in self.lastw.get(r, {}).items():
                need[sname] = max(need.get(sname, 0), c)
        for w in writes:
            for sname, c in self.lastw.get(w, {}).items():
                need[sname] = max(need.get(sname, 0), c)
            for sname, c in self.readers.get(w, {}).items():
                need[sname] = max(need.get(sname, 0), c)
        for sname in sorted(need):
            val = need[sname]
            if sname in self.dma_sems:
                val = max(val, self.dma_sems[sname])
            self._emit_wait(e, sname, val)

    def op(self, e, reads=(), writes=(), emit=None, inc=1):
        reads = [rk(r) for r in reads]
        writes = [rk(w) for w in writes]
        self._dep_waits(e, reads, writes)
        self.eng_sem[e] += inc
        cnt = self.eng_sem[e]

        def f(eng, plan=self, emit=emit, inc=inc, e=e):
            ins = emit(eng)
            ins.then_inc(plan._sem_handles[e], inc)
        self.cl[e].append(f)
        for w in writes:
            self.lastw[w] = {e: cnt}
            self.readers[w] = {}
        for r in reads:
            self.readers.setdefault(r, {})[e] = cnt
        return cnt

    def dma(self, e, sem, reads=(), writes=(), emit=None):
        reads = [rk(r) for r in reads]
        writes = [rk(w) for w in writes]
        self._dep_waits(e, reads, writes)
        self.dma_sems[sem] += 16
        cnt = self.dma_sems[sem]

        def f(eng, plan=self, emit=emit, sem=sem):
            ins = emit(eng)
            ins.then_inc(plan._sem_handles[sem], 16)
        self.cl[e].append(f)
        for w in writes:
            self.lastw[w] = {sem: cnt}
            self.readers[w] = {}
        for r in reads:
            self.readers.setdefault(r, {})[sem] = cnt
        return cnt

    def raw(self, e, f):
        self.cl[e].append(f)

    def barrier(self):
        snap_e = dict(self.eng_sem)
        snap_d = dict(self.dma_sems)
        for e in ENGMAP:
            for s, v in snap_e.items():
                if s != e:
                    self._emit_wait(e, s, v)
            for s, v in snap_d.items():
                self._emit_wait(e, s, v)

    def switch(self, engines, index_fn, n, body):
        import copy
        for e in engines:
            self.cl[e].append(("SW_BEGIN", index_fn, n))
        snap = (dict(self.eng_sem), dict(self.dma_sems))
        lw_snap = copy.deepcopy(self.lastw)
        rd_snap = copy.deepcopy(self.readers)
        wm_snap = {e: dict(self.wm[e]) for e in ENGMAP}
        end_state = None
        end_lw, end_rd = [], []
        for i in range(n):
            if i > 0:
                for e in engines:
                    self.cl[e].append(("SW_NEXT",))
                self.eng_sem, self.dma_sems = dict(snap[0]), dict(snap[1])
                self.lastw = copy.deepcopy(lw_snap)
                self.readers = copy.deepcopy(rd_snap)
                for e in ENGMAP:
                    self.wm[e] = dict(wm_snap[e])
            body(i)
            st = (dict(self.eng_sem), dict(self.dma_sems))
            if end_state is None:
                end_state = st
            else:
                assert st == end_state, f"switch case {i}: sem deltas differ"
            end_lw.append(copy.deepcopy(self.lastw))
            end_rd.append(copy.deepcopy(self.readers))
        for e in engines:
            self.cl[e].append(("SW_END",))
        mlw, mrd = {}, {}
        for lw in end_lw:
            for k, d in lw.items():
                t = mlw.setdefault(k, {})
                for sn, c in d.items():
                    t[sn] = max(t.get(sn, 0), c)
        for rd in end_rd:
            for k, d in rd.items():
                t = mrd.setdefault(k, {})
                for sn, c in d.items():
                    t[sn] = max(t.get(sn, 0), c)
        self.lastw, self.readers = mlw, mrd
        for e in ENGMAP:
            self.wm[e] = dict(wm_snap[e])

    def run_engine(self, e, eng, nc):
        pid = None
        gens = []
        for item in self.cl[e]:
            if callable(item):
                item(eng)
            elif item[0] == "SW_BEGIN":
                if pid is None:
                    pid = nc.partition_id([ENGMAP[e]])
                _, index_fn, n = item
                g = iter(eng.Switch(index_fn(pid), n))
                gens.append(g)
                next(g)
            elif item[0] == "SW_NEXT":
                next(gens[-1])
            else:
                try:
                    next(gens[-1])
                    raise AssertionError("switch not exhausted")
                except StopIteration:
                    pass
                gens.pop()



def _gu_lay(w):
    return np.ascontiguousarray(
        w.reshape(16, 128, 3, 256).transpose(1, 2, 0, 3).reshape(128, 12288))


def _d_lay(w):
    return np.ascontiguousarray(
        w.reshape(6, 128, 16, 128).transpose(1, 2, 0, 3).reshape(128, 12288))


def _core_token_idx(c):
    if c % 2 == 0:
        return np.concatenate([np.arange(0, 512), np.arange(1536, 2048)])
    return np.concatenate([np.arange(512, 1024), np.arange(1024, 1536)])


def prep_host(inputs):
    x = np.asarray(inputs['L0_attn_hidden_states'])
    wq = np.asarray(inputs['L0_attn_Wq'])
    wk = np.asarray(inputs['L0_attn_Wk'])
    wv = np.asarray(inputs['L0_attn_Wv'])
    wo = np.asarray(inputs['L0_attn_Wo'])
    qnw = np.ascontiguousarray(np.asarray(inputs['L0_attn_q_norm_w']).reshape(128, 1)).astype(np.float32)
    knw = np.ascontiguousarray(np.asarray(inputs['L0_attn_k_norm_w']).reshape(128, 1)).astype(np.float32)
    rw = np.asarray(inputs['L0_ffn_router_w'])
    wg = np.asarray(inputs['L0_ffn_w_gate'])
    wu = np.asarray(inputs['L0_ffn_w_up'])
    wd = np.asarray(inputs['L0_ffn_w_down'])

    wffn = np.concatenate(
        [np.concatenate([_gu_lay(wg[e]), _gu_lay(wu[e]), _d_lay(wd[e])], 1)
         for e in range(NEXP)], 1)

    def head_lay(w, nh):
        return w.reshape(16, 128, nh, 128).transpose(1, 2, 0, 3).reshape(128, nh * 2048)
    wqkv = np.ascontiguousarray(
        np.concatenate([head_lay(wk, 4), head_lay(wv, 4), head_lay(wq, 32)], 1))

    wo_r = np.ascontiguousarray(
        wo.reshape(32, 128, 16, 128).transpose(1, 0, 2, 3)
          .reshape(128, 4, 8, 16, 128).transpose(0, 1, 3, 2, 4).reshape(128, 65536))

    rtw = np.ascontiguousarray(
        rw.T.reshape(16, 128, 32).transpose(1, 0, 2).reshape(128, 512)).astype(np.float32)

    ones_col = np.ones((128, 1), np.float32)
    ones_row = np.ones((1, 128), np.float32)
    identb = np.eye(128, dtype=np.float32).astype(ml_dtypes.bfloat16)
    identf = np.eye(128, dtype=np.float32)
    sel = np.zeros((32, 4096), np.float32)
    for e in range(32):
        sel[e, 128 * e:128 * (e + 1)] = 1.0
    sel = sel.astype(ml_dtypes.bfloat16)
    tri = np.zeros((128, 2048), np.float32)
    p = np.arange(128)[:, None]
    j = np.arange(512)[None, :]
    for qc in range(4):
        tri[:, 512 * qc:512 * (qc + 1)] = np.where(j > 128 * qc + p, NEG, 0.0)
    tri = tri.astype(ml_dtypes.bfloat16)

    inv_freq = 1.0 / (1e6 ** (np.arange(0, 128, 2, dtype=np.float64) / 128.0))

    per_core = []
    for c in range(NCORES):
        b = c // 2
        idx = _core_token_idx(c)
        xT = np.ascontiguousarray(x[b, idx, :].T).astype(np.float32)
        ang = idx[None, :].astype(np.float64) * inv_freq[:, None]
        ang = np.concatenate([ang, ang], 0)
        per_core.append(dict(
            xT=xT, wffn=wffn, wqkv=wqkv, wo=wo_r, rtw=rtw,
            cosT=np.cos(ang).astype(np.float32), sinT=np.sin(ang).astype(np.float32),
            tri=tri, qnw=qnw, knw=knw, ones_col=ones_col, ones_row=ones_row,
            ones_col_bf=ones_col.astype(ml_dtypes.bfloat16),
            identb=identb, identf=identf, sel=sel))
    return per_core


def reassemble(results):
    out = np.empty((4, 2048, 2048), np.float32)
    for c in range(NCORES):
        idx = _core_token_idx(c)
        out[c // 2, idx, :] = results[c]['y'].T
    return out



def build_program(phases=('ffn0', 'attn', 'ffn1'), n_exp=NEXP, n_heads=NH, dbg=()):
    nc = bass.Bass("TRN2", target_bir_lowering=False, debug=False)
    P = Plan()

    d_xT = nc.dram_tensor("xT", [D, T], F32, kind="ExternalInput").ap()
    d_wffn = nc.dram_tensor("wffn", [128, NEXP * 36864], F32, kind="ExternalInput").ap()
    d_wqkv = nc.dram_tensor("wqkv", [128, 40 * 2048], F32, kind="ExternalInput").ap()
    d_wo = nc.dram_tensor("wo", [128, 65536], F32, kind="ExternalInput").ap()
    d_rtw = nc.dram_tensor("rtw", [128, 512], F32, kind="ExternalInput").ap()
    d_cos = nc.dram_tensor("cosT", [128, T], F32, kind="ExternalInput").ap()
    d_sin = nc.dram_tensor("sinT", [128, T], F32, kind="ExternalInput").ap()
    d_tri = nc.dram_tensor("tri", [128, 2048], BF16, kind="ExternalInput").ap()
    d_qnw = nc.dram_tensor("qnw", [128, 1], F32, kind="ExternalInput").ap()
    d_knw = nc.dram_tensor("knw", [128, 1], F32, kind="ExternalInput").ap()
    d_onesc = nc.dram_tensor("ones_col", [128, 1], F32, kind="ExternalInput").ap()
    d_onesb = nc.dram_tensor("ones_col_bf", [128, 1], BF16, kind="ExternalInput").ap()
    d_onesr = nc.dram_tensor("ones_row", [1, 128], F32, kind="ExternalInput").ap()
    d_identb = nc.dram_tensor("identb", [128, 128], BF16, kind="ExternalInput").ap()
    d_identf = nc.dram_tensor("identf", [128, 128], F32, kind="ExternalInput").ap()
    d_sel = nc.dram_tensor("sel", [32, 4096], BF16, kind="ExternalInput").ap()
    d_y = nc.dram_tensor("y", [D, T], F32, kind="ExternalOutput").ap()
    d_cc_in = nc.dram_tensor("cc_in", [128, 8192], BF16, kind="Internal").ap()
    d_cc_out = nc.dram_tensor("cc_out", [1024, 8192], BF16, kind="Internal",
                              addr_space="Shared").ap()
    d_dbg = {}
    for (name, shape, dt) in dbg:
        d_dbg[name] = nc.dram_tensor("dbg_" + name, list(shape), dt,
                                     kind="ExternalOutput").ap()

    for s in ['sConst', 'sX', 'sG0', 'sG1', 'sU0', 'sU1', 'sD0', 'sD1', 'sD2',
              'sCI', 'sKV0', 'sKV1', 'sKV2', 'sKV3', 'sY', 'sSel', 'sRope', 'sCC']:
        P.new_dma_sem(s)

    with contextlib.ExitStack() as top:
        EE = top.enter_context
        x = [EE(nc.sbuf_tensor(f"x{m}", [128, T], F32)) for m in range(KD)]
        xn = [EE(nc.sbuf_tensor(f"xn{m}", [128, T], BF16)) for m in range(KD)]
        G = [EE(nc.sbuf_tensor(f"Gw{i}", [128, 4096], BF16)) for i in range(2)]
        onesc = EE(nc.sbuf_tensor("onesc", [128, 1], F32))
        onesb = EE(nc.sbuf_tensor("onesb", [128, 1], BF16))
        onesr = EE(nc.sbuf_tensor("onesr", [1, 128], F32))
        identb = EE(nc.sbuf_tensor("identb_s", [128, 128], BF16))
        identf = EE(nc.sbuf_tensor("identf_s", [128, 128], F32))
        qnw = EE(nc.sbuf_tensor("qnw_s", [128, 1], F32))
        qnws = EE(nc.sbuf_tensor("qnw_sc", [128, 1], F32))
        knw = EE(nc.sbuf_tensor("knw_s", [128, 1], F32))
        rtw = EE(nc.sbuf_tensor("rtw_s", [128, 512], F32))

        for (tile, src) in [(onesc, d_onesc), (onesb, d_onesb), (onesr, d_onesr), (identb, d_identb),
                            (identf, d_identf), (qnw, d_qnw),
                            (knw, d_knw), (rtw, d_rtw)]:
            P.dma('S', 'sConst', writes=[tile],
                  emit=lambda eng, t=tile, s=src: eng.dma_start(t[:, :], s[:, :]))
        P.op('V', reads=[qnw], writes=[qnws],
             emit=lambda eng: eng.tensor_scalar(qnws[:, 0:1], qnw[:, 0:1], SCALING, None, ALU.mult))
        for m in range(KD):
            P.dma('S', 'sX', writes=[x[m]],
                  emit=lambda eng, m=m: eng.dma_start(x[m][:, :], d_xT[128 * m:128 * (m + 1), :]))

        def ln_sublayer(sc, with_router):
            sqb, xs32, Ab, Bb, rows = sc['sqb'], sc['xs32'], sc['Ab'], sc['Bb'], sc['rows']
            psS, psB = sc['psS'], sc['psB']
            mean_r, ve_r, rstd_r, t_r = rows
            sum_r, ssq_r, sr_r = mean_r, ve_r, t_r
            for m in range(KD):
                P.op('A', reads=[x[m]], writes=[xn[m]],
                     emit=lambda eng, m=m: eng.copy(xn[m][:, :], x[m][:, :]))
                P.op('A', reads=[xn[m]], writes=[sqb[m % 2]],
                     emit=lambda eng, m=m: eng.activation(sqb[m % 2][:, :], xn[m][:, :], AF.Square))
                for th in range(2):
                    P.op('T', reads=[xn[m], onesb], writes=[psS[th]],
                         emit=lambda eng, m=m, th=th: eng.matmul(
                             psS[th][0:1, :], onesb[:, :], xn[m][:, TH * th:TH * (th + 1)],
                             start=(m == 0), stop=(m == KD - 1)))
                    P.op('T', reads=[sqb[m % 2], onesb], writes=[psS[2 + th]],
                         emit=lambda eng, m=m, th=th: eng.matmul(
                             psS[2 + th][0:1, :], onesb[:, :], sqb[m % 2][:, TH * th:TH * (th + 1)],
                             start=(m == 0), stop=(m == KD - 1)))
            for th in range(2):
                P.op('A', reads=[psS[th]], writes=[sum_r],
                     emit=lambda eng, th=th: eng.copy(sum_r[0:1, TH * th:TH * (th + 1)], psS[th][0:1, :]))
                P.op('A', reads=[psS[2 + th]], writes=[ssq_r],
                     emit=lambda eng, th=th: eng.copy(ssq_r[0:1, TH * th:TH * (th + 1)], psS[2 + th][0:1, :]))
            P.op('V', reads=[sum_r], writes=[mean_r],
                 emit=lambda eng: eng.tensor_scalar(mean_r[0:1, :], sum_r[0:1, :], 1.0 / D, None, ALU.mult))
            P.op('V', reads=[mean_r], writes=[t_r],
                 emit=lambda eng: eng.tensor_tensor(t_r[0:1, :], mean_r[0:1, :], mean_r[0:1, :], ALU.mult))
            P.op('V', reads=[ssq_r], writes=[ve_r],
                 emit=lambda eng: eng.tensor_scalar(ve_r[0:1, :], ssq_r[0:1, :], 1.0 / D, EPS_LN, ALU.mult, ALU.add))
            P.op('V', reads=[ve_r, t_r], writes=[ve_r],
                 emit=lambda eng: eng.tensor_tensor(ve_r[0:1, :], ve_r[0:1, :], t_r[0:1, :], ALU.subtract))
            rsqrt_row(ve_r, rstd_r, t_r)
            P.op('V', reads=[mean_r, rstd_r], writes=[mean_r],
                 emit=lambda eng: eng.tensor_tensor(mean_r[0:1, :], mean_r[0:1, :], rstd_r[0:1, :], ALU.mult))
            for th in range(2):
                P.op('T', reads=[rstd_r, onesr], writes=[psB[0]],
                     emit=lambda eng, th=th: eng.matmul(psB[0][:, :], onesr[0:1, :],
                                                        rstd_r[0:1, TH * th:TH * (th + 1)], start=True, stop=True))
                P.op('A', reads=[psB[0]], writes=[Ab],
                     emit=lambda eng, th=th: eng.copy(Ab[:, TH * th:TH * (th + 1)], psB[0][:, :]))
                P.op('T', reads=[mean_r, onesr], writes=[psB[1]],
                     emit=lambda eng, th=th: eng.matmul(psB[1][:, :], onesr[0:1, :],
                                                        mean_r[0:1, TH * th:TH * (th + 1)], start=True, stop=True))
                P.op('A', reads=[psB[1]], writes=[Bb],
                     emit=lambda eng, th=th: eng.copy(Bb[:, TH * th:TH * (th + 1)], psB[1][:, :]))
            for m in range(KD):
                pp = m % 2
                P.op('V', reads=[x[m], Ab], writes=[xs32[pp]],
                     emit=lambda eng, m=m, pp=pp: eng.tensor_tensor(
                         xs32[pp][:, :], x[m][:, :], Ab[:, :], ALU.mult))
                if with_router:
                    P.op('V', reads=[xs32[pp], Bb], writes=[xs32[pp]],
                         emit=lambda eng, pp=pp: eng.tensor_tensor(
                             xs32[pp][:, :], xs32[pp][:, :], Bb[:, :], ALU.subtract))
                    P.op('A', reads=[xs32[pp]], writes=[xn[m]],
                         emit=lambda eng, m=m, pp=pp: eng.copy(xn[m][:, :], xs32[pp][:, :]))
                    psL = sc['psL']
                    for th in range(2):
                        P.op('T', reads=[xs32[pp], rtw], writes=[psL[th]],
                             emit=lambda eng, m=m, pp=pp, th=th: eng.matmul(
                                 psL[th][0:32, :], rtw[:, 32 * m:32 * (m + 1)],
                                 xs32[pp][:, TH * th:TH * (th + 1)],
                                 start=(m == 0), stop=(m == KD - 1)))
                else:
                    P.op('V', reads=[xs32[pp], Bb], writes=[xn[m]],
                         emit=lambda eng, m=m, pp=pp: eng.tensor_tensor(
                             xn[m][:, :], xs32[pp][:, :], Bb[:, :], ALU.subtract))

        def rsqrt_row(ve_r, rstd_r, t_r, n=T):
            P.op('A', reads=[ve_r], writes=[t_r],
                 emit=lambda eng: eng.activation(t_r[0:1, :n], ve_r[0:1, :n], AF.Ln))
            P.op('A', reads=[t_r], writes=[rstd_r],
                 emit=lambda eng: eng.activation(rstd_r[0:1, :n], t_r[0:1, :n], AF.Exp, scale=-0.5))

        def bc8(tile):
            h = tile[:, 0:8]
            return bass.AP(tensor=h.tensor, offset=h.offset,
                           ap=[[h.ap[0][0], 128], [1, 8], [0, 32]])

        def r3(ap):
            return ap.rearrange("p (a b) -> p a b", a=8)

        def router_topk_part1(sc):
            lg, ltm, etm, ptk, wtm, wfmb = sc['lg'], sc['ltm'], sc['etm'], sc['ptk'], sc['wtm'], sc['wfmb']
            psL, psT = sc['psL'], sc['psB'][0]
            m8, ss8, rs8 = sc['small'][0:3]
            for th in range(2):
                P.op('A', reads=[psL[th]], writes=[lg],
                     emit=lambda eng, th=th: eng.copy(lg[0:32, TH * th:TH * (th + 1)], psL[th][0:32, :]))
            for c in range(8):
                P.op('T', reads=[lg, identf], writes=[psT],
                     emit=lambda eng, c=c: eng.transpose(psT[:, 0:32], lg[0:32, 128 * c:128 * (c + 1)],
                                                         identf[0:32, 0:32]))
                P.op('A', reads=[psT], writes=[ltm],
                     emit=lambda eng, c=c: eng.copy(ltm[:, 32 * c:32 * (c + 1)], psT[:, 0:32]))
            P.op('V', reads=[ltm], writes=[m8],
                 emit=lambda eng: eng.reduce_max(m8[:, 0:8], r3(ltm[:, :]), axis=AX.X))
            P.op('V', reads=[ltm, m8], writes=[ptk],
                 emit=lambda eng: eng.tensor_tensor(r3(ptk[:, :]), r3(ltm[:, :]), bc8(m8), ALU.subtract))
            P.op('A', reads=[ptk], writes=[etm],
                 emit=lambda eng: eng.activation(etm[:, :], ptk[:, :], AF.Exp))
            P.op('A', reads=[etm], writes=[ptk],
                 emit=lambda eng: eng.copy(ptk[:, :], etm[:, :]))
            P.op('V', writes=[ss8],
                 emit=lambda eng: eng.memset(ss8[:, 0:8], 0.0))
            for i in range(8):
                P.op('V', reads=[ptk], writes=[m8],
                     emit=lambda eng: eng.reduce_max(m8[:, 0:8], r3(ptk[:, :]), axis=AX.X))
                P.op('V', reads=[m8, ss8], writes=[ss8],
                     emit=lambda eng: eng.tensor_tensor(ss8[:, 0:8], ss8[:, 0:8], m8[:, 0:8], ALU.add))
                if i < 7:
                    P.op('V', reads=[ptk, m8], writes=[ltm],
                         emit=lambda eng: eng.tensor_tensor(r3(ltm[:, :]), r3(ptk[:, :]), bc8(m8), ALU.is_ge))
                    P.op('V', reads=[ltm, ptk], writes=[wtm],
                         emit=lambda eng: eng.tensor_tensor(wtm[:, :], ltm[:, :], ptk[:, :], ALU.mult))
                    P.op('V', reads=[wtm, ptk], writes=[ptk],
                         emit=lambda eng: eng.tensor_tensor(ptk[:, :], ptk[:, :], wtm[:, :], ALU.subtract))
            P.op('V', reads=[ss8], writes=[rs8],
                 emit=lambda eng: eng.reciprocal(rs8[:, 0:8], ss8[:, 0:8]))
            P.op('V', reads=[rs8], writes=[rs8],
                 emit=lambda eng: eng.tensor_scalar(rs8[:, 0:8], rs8[:, 0:8], 0.5, None, ALU.mult))
            P.op('V', reads=[etm, m8], writes=[ltm],
                 emit=lambda eng: eng.tensor_tensor(r3(ltm[:, :]), r3(etm[:, :]), bc8(m8), ALU.is_ge))
            P.op('V', reads=[ltm, etm], writes=[wtm],
                 emit=lambda eng: eng.tensor_tensor(wtm[:, :], etm[:, :], ltm[:, :], ALU.mult))
            P.op('V', reads=[wtm, rs8], writes=[wtm],
                 emit=lambda eng: eng.tensor_tensor(r3(wtm[:, :]), r3(wtm[:, :]), bc8(rs8), ALU.mult))

        def router_topk_part2(sc):
            wtm, wfmb, psT = sc['wtm'], sc['wfmb'], sc['psB'][0]
            for c in range(8):
                P.op('T', reads=[wtm, identf], writes=[psT],
                     emit=lambda eng, c=c: eng.transpose(psT[0:32, 0:128], wtm[:, 32 * c:32 * (c + 1)],
                                                         identf[:, :]))
                P.op('A', reads=[psT], writes=[wfmb],
                     emit=lambda eng, c=c: eng.copy(wfmb[0:32, 128 * c:128 * (c + 1)], psT[0:32, 0:128]))

        def ffn_pass(pi):
            tag = f'F{pi}_'
            with contextlib.ExitStack() as scx:
                E = scx.enter_context
                U = [E(nc.sbuf_tensor(f"{tag}U{i}", [128, 4096], BF16)) for i in range(2)]
                Dw = [E(nc.sbuf_tensor(f"{tag}D{i}", [128, 1536], BF16)) for i in range(3)]
                h = [E(nc.sbuf_tensor(f"{tag}h{i}", [128, T], BF16)) for i in range(6)]
                wbc = [E(nc.sbuf_tensor(f"{tag}wbc{i}", [128, T], BF16)) for i in range(2)]
                sel_sb = E(nc.sbuf_tensor(f"{tag}sel", [32, 4096], BF16))
                xs32 = [E(nc.sbuf_tensor(f"{tag}xs{i}", [128, T], F32)) for i in range(2)]
                Ab = E(nc.sbuf_tensor(f"{tag}Ab", [128, T], F32))
                Bb = E(nc.sbuf_tensor(f"{tag}Bb", [128, T], F32))
                rows = [E(nc.sbuf_tensor(f"{tag}row{i}", [1, T], F32)) for i in range(4)]
                lg = E(nc.sbuf_tensor(f"{tag}lg", [32, T], F32))
                ltm = E(nc.sbuf_tensor(f"{tag}ltm", [128, 256], F32))
                etm = E(nc.sbuf_tensor(f"{tag}etm", [128, 256], F32))
                ptk = E(nc.sbuf_tensor(f"{tag}ptk", [128, 256], F32))
                wtm = E(nc.sbuf_tensor(f"{tag}wtm", [128, 256], F32))
                wfmb = E(nc.sbuf_tensor(f"{tag}wfmb", [32, T], BF16))
                small = [E(nc.sbuf_tensor(f"{tag}sm{i}", [128, 32], F32)) for i in range(3)]
                psG = [E(nc.psum_tensor(f"{tag}psG{i}", [128, 512], F32)) for i in range(4)]
                psU = [E(nc.psum_tensor(f"{tag}psU{i}", [128, 512], F32)) for i in range(4)]
                sc = dict(sqb=h[0:2], xs32=xs32, Ab=Ab, Bb=Bb, rows=rows, lg=lg, ltm=ltm,
                          etm=etm, ptk=ptk, wtm=wtm, wfmb=wfmb, small=small,
                          psS=psG, psB=psU[:2], psL=psU[2:4])

                P.dma('S', 'sSel', writes=[sel_sb],
                      emit=lambda eng: eng.dma_start(sel_sb[:, :], d_sel[:, :]))

                loadq = []
                lidx = {}

                def push_load(key, buf, sem, lo, hi):
                    lidx[key] = len(loadq)

                    def planit():
                        P.dma('P', sem, writes=[buf],
                              emit=lambda eng: eng.dma_start(buf[:, 0:hi - lo], d_wffn[:, lo:hi]))
                    loadq.append(planit)

                def push_G(e, g):
                    sl = (3 * e + g) % 2
                    base = 36864 * e
                    push_load(('G', e, g), G[sl], f'sG{sl}', base + 4096 * g, base + 4096 * (g + 1))

                def push_U(e, g):
                    sl = (3 * e + g) % 2
                    base = 36864 * e
                    push_load(('U', e, g), U[sl], f'sU{sl}',
                              base + 12288 + 4096 * g, base + 12288 + 4096 * (g + 1))

                def push_D(e, q):
                    sl = (8 * e + q) % 3
                    base = 36864 * e
                    push_load(('D', e, q), Dw[sl], f'sD{sl}',
                              base + 24576 + 1536 * q, base + 24576 + 1536 * (q + 1))

                push_G(0, 0); push_U(0, 0); push_G(0, 1); push_U(0, 1)
                push_G(0, 2); push_U(0, 2)
                for e in range(n_exp):
                    if e + 1 < n_exp:
                        push_G(e + 1, 0)
                    for q in range(8):
                        push_D(e, q)
                    if e + 1 < n_exp:
                        push_U(e + 1, 0)
                        for g in (1, 2):
                            push_G(e + 1, g)
                            push_U(e + 1, g)
                pos = [0]

                def pump(n):
                    n = min(n, len(loadq))
                    while pos[0] < n:
                        loadq[pos[0]]()
                        pos[0] += 1

                pump(4)
                ln_sublayer(sc, True)
                router_topk_part1(sc)

                def gate_groups(e, g):
                    P1 = bank1(e)
                    sl = (3 * e + g) % 2
                    pump(lidx[('G', e, g)] + 3)
                    for b in range(4):
                        mh, th = b // 2, b % 2

                        def em(eng, sl=sl, P1=P1, mh=mh, th=th):
                            for k in range(16):
                                mm = eng.matmul(P1[2 * mh + th][:, :],
                                                G[sl][:, 256 * k + 128 * mh:256 * k + 128 * (mh + 1)],
                                                xn[k][:, TH * th:TH * (th + 1)],
                                                start=(k == 0), stop=(k == 15))
                            return mm
                        P.op('T', reads=[G[sl]] + xn, writes=[P1[2 * mh + th]], emit=em)

                def silu_evacs(e, g):
                    P1 = bank1(e)
                    for b in range(4):
                        mh, th = b // 2, b % 2
                        hm = 2 * g + mh
                        P.op('A', reads=[P1[2 * mh + th]], writes=[h[hm]],
                             emit=lambda eng, hm=hm, mh=mh, th=th, P1=P1: eng.activation(
                                 h[hm][:, TH * th:TH * (th + 1)], P1[2 * mh + th][:, :], AF.Silu))

                def up_groups(e, g):
                    P2 = bank2(e)
                    sl = (3 * e + g) % 2
                    pump(lidx[('U', e, g)] + 3)
                    for b in range(4):
                        mh, th = b // 2, b % 2
                        hm = 2 * g + mh

                        def em(eng, sl=sl, P2=P2, mh=mh, th=th):
                            for k in range(16):
                                mm = eng.matmul(P2[2 * mh + th][:, :],
                                                U[sl][:, 256 * k + 128 * mh:256 * k + 128 * (mh + 1)],
                                                xn[k][:, TH * th:TH * (th + 1)],
                                                start=(k == 0), stop=(k == 15))
                            return mm
                        P.op('T', reads=[U[sl]] + xn, writes=[P2[2 * mh + th]], emit=em)
                        P.op('V', reads=[P2[2 * mh + th], h[hm]], writes=[h[hm]],
                             emit=lambda eng, hm=hm, mh=mh, th=th, P2=P2: eng.tensor_tensor(
                                 h[hm][:, TH * th:TH * (th + 1)], P2[2 * mh + th][:, :],
                                 h[hm][:, TH * th:TH * (th + 1)], ALU.mult))
                        if e > 0 and th == 1:
                            P.op('P', reads=[h[hm], wbc[e % 2]], writes=[h[hm]],
                                 emit=lambda eng, hm=hm, e=e: eng.tensor_tensor(
                                     h[hm][:, :], h[hm][:, :], wbc[e % 2][:, :], ALU.mult))

                def bank1(e):
                    return psG if e % 2 == 0 else psU

                def bank2(e):
                    return psU if e % 2 == 0 else psG

                def wbc_mm(e):
                    b = e % 2
                    pt = bank1(e)
                    for th in range(2):
                        P.op('T', reads=[wfmb, sel_sb], writes=[pt[th]],
                             emit=lambda eng, e=e, th=th, pt=pt: eng.matmul(
                                 pt[th][:, :], sel_sb[0:32, 128 * e:128 * (e + 1)],
                                 wfmb[0:32, TH * th:TH * (th + 1)], start=True, stop=True))
                        P.op('A', reads=[pt[th]], writes=[wbc[b]],
                             emit=lambda eng, b=b, th=th, pt=pt: eng.copy(wbc[b][:, TH * th:TH * (th + 1)], pt[th][:, :]))

                gate_groups(0, 0)
                for e in range(n_exp):
                    P1 = bank1(e)
                    for g in range(3):
                        if g > 0:
                            gate_groups(e, g)
                        silu_evacs(e, g)
                        up_groups(e, g)
                    if e == 0:
                        router_topk_part2(sc)
                        wbc_mm(0)
                        for hm in range(6):
                            P.op('P', reads=[h[hm], wbc[0]], writes=[h[hm]],
                                 emit=lambda eng, hm=hm: eng.tensor_tensor(
                                     h[hm][:, :], h[hm][:, :], wbc[0][:, :], ALU.mult))
                    if e + 1 < n_exp:
                        wbc_mm(e + 1)
                        gate_groups(e + 1, 0)
                    for m2 in range(16):
                        pump(lidx[('D', e, m2 // 2)] + 3)
                        sl = (8 * e + m2 // 2) % 3

                        def em_down(eng, m2=m2, sl=sl, P1=P1):
                            for k in range(6):
                                for th in range(2):
                                    mm = eng.matmul(P1[2 * (m2 % 2) + th][:, :],
                                                    Dw[sl][:, 768 * (m2 % 2) + 128 * k:768 * (m2 % 2) + 128 * (k + 1)],
                                                    h[k][:, TH * th:TH * (th + 1)],
                                                    start=(k == 0), stop=(k == 5))
                            return mm
                        P.op('T', reads=[Dw[sl]] + h, writes=[P1[2 * (m2 % 2)], P1[2 * (m2 % 2) + 1]],
                             emit=em_down)
                        for th in range(2):
                            P.op('V', reads=[P1[2 * (m2 % 2) + th], x[m2]], writes=[x[m2]],
                                 emit=lambda eng, m2=m2, th=th, P1=P1: eng.tensor_tensor(
                                     x[m2][:, TH * th:TH * (th + 1)], P1[2 * (m2 % 2) + th][:, :],
                                     x[m2][:, TH * th:TH * (th + 1)], ALU.add))
                for name in list(d_dbg):
                    if name.startswith(f'ffn{pi}'):
                        dump_dbg(name)
                P.barrier()

        def dump_dbg(name):
            tgt = name.split('_', 1)[1]
            src = {'x': x, 'xn': xn}[tgt]
            for m in range(KD):
                P.dma('S', 'sY', reads=[src[m]],
                      emit=lambda eng, m=m, name=name: eng.dma_start(
                          d_dbg[name][128 * m:128 * (m + 1), :], src[m][:, :]))

        def attn_pass():
            tag = 'AT_'
            with contextlib.ExitStack() as lsc:
                E = lsc.enter_context
                xs2 = [E(nc.sbuf_tensor(f"{tag}lxs{i}", [128, T], F32)) for i in range(2)]
                sq2 = [E(nc.sbuf_tensor(f"{tag}lsq{i}", [128, T], BF16)) for i in range(2)]
                Ab2 = E(nc.sbuf_tensor(f"{tag}lAb", [128, T], F32))
                Bb2 = E(nc.sbuf_tensor(f"{tag}lBb", [128, T], F32))
                lrows = [E(nc.sbuf_tensor(f"{tag}lrow{i}", [1, T], F32)) for i in range(4)]
                psS = [E(nc.psum_tensor(f"{tag}lpsS{i}", [128, 512], F32)) for i in range(4)]
                psB = [E(nc.psum_tensor(f"{tag}lpsB{i}", [128, 512], F32)) for i in range(2)]
                ln_sublayer(dict(sqb=sq2, xs32=xs2, Ab=Ab2, Bb=Bb2, rows=lrows,
                                 psS=psS, psB=psB, psL=None), False)
            P.barrier()

            with contextlib.ExitStack() as scx:
                E = scx.enter_context
                cosT = E(nc.sbuf_tensor(f"{tag}cos", [128, T], F32))
                sinT = E(nc.sbuf_tensor(f"{tag}sin", [128, T], F32))
                sqA = E(nc.sbuf_tensor(f"{tag}sq", [128, T], F32))
                rb_sb = E(nc.sbuf_tensor(f"{tag}rb", [128, TH], F32))
                rows = [E(nc.sbuf_tensor(f"{tag}row{i}", [1, T], F32)) for i in range(3)]
                tri_s = E(nc.sbuf_tensor(f"{tag}tri", [128, 2048], BF16))
                kb = E(nc.sbuf_tensor(f"{tag}kb", [128, T], BF16))
                vb = E(nc.sbuf_tensor(f"{tag}vb", [128, T], BF16))
                vt = E(nc.sbuf_tensor(f"{tag}vt", [128, T], BF16))
                K = [E(nc.sbuf_tensor(f"{tag}K{g}", [128, 2048], BF16)) for g in range(4)]
                V = [E(nc.sbuf_tensor(f"{tag}V{g}", [128, 2048], BF16)) for g in range(4)]
                QO = [E(nc.sbuf_tensor(f"{tag}QO{i}", [128, T], BF16)) for i in range(8)]
                e_sb = E(nc.sbuf_tensor(f"{tag}esb", [128, 2048], BF16))
                eTq = E(nc.sbuf_tensor(f"{tag}eTq", [128, 2048], BF16))
                ssum = E(nc.sbuf_tensor(f"{tag}ssum", [128, 4], F32))
                rsc = E(nc.sbuf_tensor(f"{tag}rsc", [128, 4], F32))
                nmx = E(nc.sbuf_tensor(f"{tag}nmx", [128, 1], F32))
                bia = E(nc.sbuf_tensor(f"{tag}bia", [128, 1], F32))
                pscore = E(nc.psum_tensor(f"{tag}pscore", [128, 2048], F32))
                ptp = E(nc.psum_tensor(f"{tag}ptp", [128, 1024], BF16))
                psO = [E(nc.psum_tensor(f"{tag}psO{i}", [128, 512], F32)) for i in range(2)]
                psx = E(nc.psum_tensor(f"{tag}psx", [128, 512], F32))
                ve_r, rstd_r, t_r = rows

                P.dma('S', 'sRope', writes=[cosT],
                      emit=lambda eng: eng.dma_start(cosT[:, :], d_cos[:, :]))
                P.dma('S', 'sRope', writes=[sinT],
                      emit=lambda eng: eng.dma_start(sinT[:, :], d_sin[:, :]))
                P.dma('S', 'sRope', writes=[tri_s],
                      emit=lambda eng: eng.dma_start(tri_s[:, :], d_tri[:, :]))

                loadq = []

                def push_wload(lo, hi, src=d_wqkv):
                    i = len(loadq)
                    sl = i % 2

                    def planit(sl=sl, lo=lo, hi=hi, src=src):
                        P.dma('P', f'sG{sl}', writes=[G[sl]],
                              emit=lambda eng: eng.dma_start(G[sl][:, 0:hi - lo], src[:, lo:hi]))
                    loadq.append(planit)
                    return i, sl

                ld_ids = {}
                for hk in range(4):
                    ld_ids[('k', hk)] = push_wload(2048 * hk, 2048 * (hk + 1))
                for hk in range(4):
                    ld_ids[('v', hk)] = push_wload(8192 + 2048 * hk, 8192 + 2048 * (hk + 1))
                for w in range(n_heads // 8):
                    for i in range(8):
                        hq = 8 * w + i
                        ld_ids[('q', hq)] = push_wload(16384 + 2048 * hq, 16384 + 2048 * (hq + 1))
                    for j4 in range(4):
                        ld_ids[('o', w, j4)] = push_wload(16384 * w + 4096 * j4,
                                                          16384 * w + 4096 * (j4 + 1), src=d_wo)
                pos = [0]

                def pump(n):
                    n = min(n, len(loadq))
                    while pos[0] < n:
                        loadq[pos[0]]()
                        pos[0] += 1

                PK = [f'{tag}pk{i}' for i in range(4)]

                def qps(par, th):
                    if par == 0:
                        return psO[th][:, :], rk(psO[th])
                    return pscore[:, 1024 + TH * th:1024 + TH * (th + 1)], PK[2 + th]

                def head_mm(sl, par):
                    aps = [qps(par, th) for th in range(2)]

                    def em(eng, sl=sl, aps=aps):
                        for k in range(16):
                            for th in range(2):
                                mm = eng.matmul(aps[th][0], G[sl][:, 128 * k:128 * (k + 1)],
                                                xn[k][:, TH * th:TH * (th + 1)],
                                                start=(k == 0), stop=(k == 15))
                        return mm
                    P.op('T', reads=[G[sl]] + xn, writes=[a[1] for a in aps], emit=em)

                def rms_rope(norm_w, dst, par, peng='P'):
                    aps = [qps(par, th) for th in range(2)]
                    for th in range(2):
                        P.op('A', reads=[aps[th][1]], writes=[vb],
                             emit=lambda eng, th=th, aps=aps: eng.activation(
                                 vb[:, TH * th:TH * (th + 1)], aps[th][0], AF.Square))
                        P.op('T', reads=[vb, onesb], writes=[PK[th]],
                             emit=lambda eng, th=th: eng.matmul(
                                 pscore[0:1, TH * th:TH * (th + 1)], onesb[:, :],
                                 vb[:, TH * th:TH * (th + 1)], start=True, stop=True))
                    P.op('V', reads=[PK[0], PK[1]], writes=[ve_r],
                         emit=lambda eng: eng.tensor_scalar(ve_r[0:1, :], pscore[0:1, 0:T],
                                                            1.0 / HD, EPS_RMS, ALU.mult, ALU.add))
                    rsqrt_row(ve_r, rstd_r, t_r)
                    for th in range(2):
                        P.op('T', reads=[rstd_r, onesr], writes=[psx],
                             emit=lambda eng, th=th: eng.matmul(psx[:, :], onesr[0:1, :],
                                                                rstd_r[0:1, TH * th:TH * (th + 1)],
                                                                start=True, stop=True))
                        P.op('A', reads=[psx], writes=[rb_sb],
                             emit=lambda eng, th=th: eng.copy(rb_sb[:, :], psx[:, :]))
                        P.op('V', reads=[aps[th][1], norm_w, rb_sb], writes=[PK[th]],
                             emit=lambda eng, th=th, aps=aps: eng.scalar_tensor_tensor(
                                 pscore[:, TH * th:TH * (th + 1)], aps[th][0], norm_w[:, 0:1],
                                 rb_sb[:, :], ALU.mult, ALU.mult))
                    so = slice(0, T)
                    kq = pscore
                    P.op('V', reads=[PK[0], PK[1], cosT], writes=[sqA],
                         emit=lambda eng: eng.tensor_tensor(sqA[0:64, :], kq[0:64, 0:T], cosT[0:64, :], ALU.mult))
                    P.op('V', reads=[PK[0], PK[1], sinT], writes=[dst],
                         emit=lambda eng: eng.scalar_tensor_tensor(
                             dst[0:64, so], kq[64:128, 0:T], -1.0, sinT[0:64, :], ALU.mult, ALU.mult))
                    P.op(peng, reads=[dst, sqA], writes=[dst],
                         emit=lambda eng: eng.tensor_tensor(dst[0:64, so], dst[0:64, so], sqA[0:64, :], ALU.add))
                    P.op('V', reads=[PK[0], PK[1], cosT], writes=[sqA],
                         emit=lambda eng: eng.tensor_tensor(sqA[64:128, :], kq[64:128, 0:T], cosT[64:128, :], ALU.mult))
                    P.op('V', reads=[PK[0], PK[1], sinT], writes=[dst],
                         emit=lambda eng: eng.tensor_tensor(
                             dst[64:128, so], kq[0:64, 0:T], sinT[64:128, :], ALU.mult))
                    P.op(peng, reads=[dst, sqA], writes=[dst],
                         emit=lambda eng: eng.tensor_tensor(dst[64:128, so], dst[64:128, so], sqA[64:128, :], ALU.add))

                for hk in range(4):
                    i, sl = ld_ids[('k', hk)]
                    pump(i + 2)
                    head_mm(sl, hk % 2)
                    rms_rope(knw, kb, hk % 2)
                    P.dma('S', 'sCI', reads=[kb], writes=['cc_in'],
                          emit=lambda eng, hk=hk: eng.dma_start(
                              d_cc_in[:, 1024 * hk:1024 * (hk + 1)], kb[:, :]))
                for hk in range(4):
                    i, sl = ld_ids[('v', hk)]
                    pump(i + 2)
                    head_mm(sl, hk % 2)
                    aps = [qps(hk % 2, th) for th in range(2)]
                    for th in range(2):
                        P.op('A', reads=[aps[th][1]], writes=[vb],
                             emit=lambda eng, th=th, aps=aps: eng.copy(vb[:, TH * th:TH * (th + 1)], aps[th][0]))

                    def em_vt(eng):
                        for i2 in range(8):
                            mm = eng.transpose(ptp[:, 128 * i2:128 * (i2 + 1)],
                                               vb[:, 128 * i2:128 * (i2 + 1)], identb[:, :])
                        return mm
                    P.op('T', reads=[vb, identb], writes=[ptp], emit=em_vt)
                    P.op('V', reads=[ptp], writes=[vt],
                         emit=lambda eng: eng.tensor_copy(vt[:, :], ptp[:, :]))
                    P.dma('S', 'sCI', reads=[vt], writes=['cc_in'],
                          emit=lambda eng, hk=hk: eng.dma_start(
                              d_cc_in[:, 4096 + 1024 * hk:4096 + 1024 * (hk + 1)], vt[:, :]))

                P.op('P', reads=['cc_in'], writes=['cc_out'], inc=1,
                     emit=lambda eng: eng.collective_compute(
                         "AllGather", ALU.bypass,
                         replica_groups=[[0, 1, 2, 3, 4, 5, 6, 7]],
                         ins=[d_cc_in[:, :]], outs=[d_cc_out[:, :]]))

                def rb_body(pair):
                    r0 = 256 * pair
                    r1 = 256 * pair + 128
                    for g in range(4):
                        for (dstc, rr, sc0) in [(0, r0, 0), (512, r1, 0), (1024, r1, 512), (1536, r0, 512)]:
                            P.dma('P', f'sKV{g}', reads=['cc_out'], writes=[K[g]],
                                  emit=lambda eng, g=g, dstc=dstc, rr=rr, sc0=sc0: eng.dma_start(
                                      K[g][:, dstc:dstc + 512],
                                      d_cc_out[rr:rr + 128, 1024 * g + sc0:1024 * g + sc0 + 512]))
                        for (dstc, rr, sc0) in [(0, r0, 0), (512, r1, 0), (1024, r1, 512), (1536, r0, 512)]:
                            P.dma('P', f'sKV{g}', reads=['cc_out'], writes=[V[g]],
                                  emit=lambda eng, g=g, dstc=dstc, rr=rr, sc0=sc0: eng.dma_start(
                                      V[g][:, dstc:dstc + 512],
                                      d_cc_out[rr:rr + 128, 4096 + 1024 * g + sc0:4096 + 1024 * g + sc0 + 512]))
                P.switch(['P'], lambda pid: pid // 2, 4, rb_body)

                def attn_head(par, hq, slot):
                    g = hq // 8
                    blocks = ([(0, 512, 0), (512, 2048, 1536)] if par == 0
                              else [(0, 1024, 512), (512, 1536, 1024)])
                    for b_idx, (qoff, ktl, dbase) in enumerate(blocks):
                        nkc = ktl // 128
                        halves = 1 if b_idx == 0 else 2
                        pend = [None]
                        for qc in range(4):
                            roff = (1024 * (qc % 2)) if b_idx == 0 else 0
                            kys = PK[roff // 512:(roff + ktl + 511) // 512]

                            def em_sc(eng, qoff=qoff, qc=qc, ktl=ktl, g=g, slot=slot, roff=roff):
                                for kc5 in range(ktl // 512):
                                    mm = eng.matmul(pscore[:, roff + 512 * kc5:roff + 512 * (kc5 + 1)],
                                                    QO[slot][:, qoff + 128 * qc:qoff + 128 * (qc + 1)],
                                                    K[g][:, 512 * kc5:512 * (kc5 + 1)],
                                                    start=True, stop=True)
                                return mm
                            P.op('T', reads=[QO[slot], K[g]], writes=kys, emit=em_sc)
                            if pend[0]:
                                pend[0]()
                            P.op('V', reads=kys + [tri_s], writes=kys,
                                 emit=lambda eng, dbase=dbase, qc=qc, roff=roff: eng.tensor_tensor(
                                     pscore[:, roff + dbase:roff + dbase + 512],
                                     pscore[:, roff + dbase:roff + dbase + 512],
                                     tri_s[:, 512 * qc:512 * (qc + 1)], ALU.add))
                            P.op('V', reads=kys, writes=[nmx],
                                 emit=lambda eng, ktl=ktl, roff=roff: eng.reduce_max(
                                     nmx[:, 0:1], pscore[:, roff:roff + ktl], axis=AX.X, negate=True))
                            P.op('A', reads=kys + [nmx], writes=[e_sb, ssum],
                                 emit=lambda eng, ktl=ktl, qc=qc, roff=roff: eng.activation(
                                     e_sb[:, 0:ktl], pscore[:, roff:roff + ktl], AF.Exp, bias=nmx[:, 0:1],
                                     scale=1.0, accum_out=ssum[:, qc:qc + 1]))

                            def mk_pend(qc=qc, nkc=nkc, halves=halves, g=g, b_idx=b_idx):
                                def pendf():
                                    for hf in range(halves):
                                        lo = 8 * hf
                                        hi = min(nkc, 8 * (hf + 1))

                                        def em_tp(eng, lo=lo, hi=hi):
                                            for i2 in range(lo, hi):
                                                mm = eng.transpose(ptp[:, 128 * (i2 - lo):128 * (i2 - lo + 1)],
                                                                   e_sb[:, 128 * i2:128 * (i2 + 1)], identb[:, :])
                                            return mm
                                        P.op('T', reads=[e_sb, identb], writes=[ptp], emit=em_tp)
                                        P.op('V', reads=[ptp], writes=[eTq],
                                             emit=lambda eng, lo=lo, hi=hi: eng.tensor_copy(
                                                 eTq[:, 128 * lo:128 * hi], ptp[:, 0:128 * (hi - lo)]))

                                        def em_av(eng, lo=lo, hi=hi, g=g, b_idx=b_idx, qc=qc, nkc=nkc):
                                            for i2 in range(lo, hi):
                                                mm = eng.matmul(psO[b_idx][:, 128 * qc:128 * (qc + 1)],
                                                                V[g][:, 128 * i2:128 * (i2 + 1)],
                                                                eTq[:, 128 * i2:128 * (i2 + 1)],
                                                                start=(i2 == 0), stop=(i2 == nkc - 1))
                                            return mm
                                        P.op('T', reads=[eTq, V[g]], writes=[psO[b_idx]], emit=em_av)
                                return pendf
                            pend[0] = mk_pend()
                        pend[0]()
                        P.op('V', reads=[ssum], writes=[rsc],
                             emit=lambda eng: eng.reciprocal(rsc[:, 0:4], ssum[:, 0:4]))

                        def em_rst(eng):
                            for qc in range(4):
                                mm = eng.transpose(psx[0:1, 128 * qc:128 * (qc + 1)],
                                                   rsc[:, qc:qc + 1], identf[:, :])
                            return mm
                        P.op('T', reads=[rsc, identf], writes=[psx], emit=em_rst)
                        P.op('A', reads=[psx], writes=[t_r],
                             emit=lambda eng: eng.copy(t_r[0:1, 0:512], psx[0:1, 0:512]))
                        P.op('T', reads=[t_r, onesr], writes=[psx],
                             emit=lambda eng: eng.matmul(psx[:, :], onesr[0:1, :], t_r[0:1, 0:512],
                                                         start=True, stop=True))
                        P.op('A', reads=[psx], writes=[rb_sb],
                             emit=lambda eng: eng.copy(rb_sb[:, :], psx[:, :]))
                        P.op('V', reads=[psO[b_idx], rb_sb], writes=[QO[slot]],
                             emit=lambda eng, b_idx=b_idx, qoff=qoff, slot=slot: eng.tensor_tensor(
                                 QO[slot][:, qoff:qoff + 512], psO[b_idx][:, :], rb_sb[:, :], ALU.mult))

                for w in range(n_heads // 8):
                    for i in range(8):
                        hq = 8 * w + i
                        li, sl = ld_ids[('q', hq)]
                        pump(li + 2)
                        head_mm(sl, i % 2)
                        rms_rope(qnws, QO[i], i % 2, 'V' if w == 0 else 'P')

                    def wave_attn(par, w=w):
                        for i in range(8):
                            attn_head(par, 8 * w + i, i)
                    P.switch(['T', 'V', 'A'], lambda pid: pid % 2, 2, wave_attn)

                    for m2 in range(16):
                        li, sl = ld_ids[('o', w, m2 // 4)]
                        pump(li + 2)
                        aps = [qps(m2 % 2, th) for th in range(2)]

                        def em_op(eng, sl=sl, m2=m2, aps=aps):
                            for kk in range(8):
                                for th in range(2):
                                    mm = eng.matmul(aps[th][0],
                                                    G[sl][:, 1024 * (m2 % 4) + 128 * kk:1024 * (m2 % 4) + 128 * (kk + 1)],
                                                    QO[kk][:, TH * th:TH * (th + 1)],
                                                    start=(kk == 0), stop=(kk == 7))
                            return mm
                        P.op('T', reads=[G[sl]] + QO, writes=[a[1] for a in aps], emit=em_op)
                        for th in range(2):
                            P.op('V', reads=[aps[th][1], x[m2]], writes=[x[m2]],
                                 emit=lambda eng, m2=m2, th=th, aps=aps: eng.tensor_tensor(
                                     x[m2][:, TH * th:TH * (th + 1)], aps[th][0],
                                     x[m2][:, TH * th:TH * (th + 1)], ALU.add))
                P.barrier()

        if 'ffn0' in phases:
            ffn_pass(0)
        if 'attn' in phases:
            attn_pass()
        if 'ffn1' in phases:
            ffn_pass(1)

        for m in range(KD):
            P.dma('S', 'sY', reads=[x[m]],
                  emit=lambda eng, m=m: eng.dma_start(d_y[128 * m:128 * (m + 1), :], x[m][:, :]))
        P.raw('S', lambda eng: eng.wait_ge(P._sem_handles['sY'], P.dma_sems['sY']))

        with contextlib.ExitStack() as sctx:
            SE = sctx.enter_context
            for s in ['T', 'V', 'A', 'P', 'S']:
                P._sem_handles[s] = SE(nc.semaphore(f"sem{s}"))
            for s in P.dma_sems:
                P._sem_handles[s] = SE(nc.semaphore(f"sem_{s}"))
            import os
            if os.environ.get('DUMP_SEMS'):
                for k, v in P._sem_handles.items():
                    print('SEMMAP', k, v)
            block = SE(nc.Block())

            @block.tensor
            def _(eng):
                P.run_engine('T', eng, nc)

            @block.vector
            def _(eng):
                P.run_engine('V', eng, nc)

            @block.scalar
            def _(eng):
                P.run_engine('A', eng, nc)

            @block.gpsimd
            def _(eng):
                P.run_engine('P', eng, nc)

            @block.sync
            def _(eng):
                P.run_engine('S', eng, nc)

    return nc


def kernel(**inputs):
    per_core = prep_host(inputs)
    nc = build_program()
    res = run_bass_kernel_spmd(nc, per_core, core_ids=list(range(NCORES)))
    return reassemble(res.results)


# version 14 (final)
# speedup vs baseline: 1.4250x; 1.0076x over previous
import contextlib
import numpy as np
import ml_dtypes

import concourse.bass as bass
import concourse.mybir as mybir
from concourse.bass_utils import run_bass_kernel_spmd

F32 = mybir.dt.float32
BF16 = mybir.dt.bfloat16
AF = mybir.ActivationFunctionType
ALU = mybir.AluOpType
AX = mybir.AxisListType
ET = mybir.EngineType

NCORES = 8
D = 2048
T = 1024
TH = 512
KD = 16
NEXP = 32
NH, NKV, HD = 32, 4, 128
SCALING = HD ** -0.5
EPS_LN = 1e-5
EPS_RMS = 1e-6
NEG = -1e30

ENGMAP = {'T': ET.PE, 'V': ET.DVE, 'A': ET.Activation, 'P': ET.Pool, 'S': ET.SP}


def rk(t):
    return t if isinstance(t, str) else t.name


class Plan:

    def __init__(self):
        self.cl = {e: [] for e in ENGMAP}
        self.eng_sem = {e: 0 for e in ENGMAP}
        self.dma_sems = {}
        self.lastw = {}
        self.readers = {}
        self.wm = {e: {} for e in ENGMAP}
        self._sem_handles = {}

    def new_dma_sem(self, name):
        self.dma_sems[name] = 0

    def _emit_wait(self, e, sem_name, val):
        if val <= 0 or self.wm[e].get(sem_name, 0) >= val:
            return
        self.wm[e][sem_name] = val

        def f(eng, plan=self, sem_name=sem_name, val=val):
            eng.wait_ge(plan._sem_handles[sem_name], val)
        self.cl[e].append(f)

    def _dep_waits(self, e, reads, writes):
        need = {}
        for r in reads:
            for sname, c in self.lastw.get(r, {}).items():
                need[sname] = max(need.get(sname, 0), c)
        for w in writes:
            for sname, c in self.lastw.get(w, {}).items():
                need[sname] = max(need.get(sname, 0), c)
            for sname, c in self.readers.get(w, {}).items():
                need[sname] = max(need.get(sname, 0), c)
        for sname in sorted(need):
            val = need[sname]
            if sname in self.dma_sems:
                val = max(val, self.dma_sems[sname])
            self._emit_wait(e, sname, val)

    def op(self, e, reads=(), writes=(), emit=None, inc=1):
        reads = [rk(r) for r in reads]
        writes = [rk(w) for w in writes]
        self._dep_waits(e, reads, writes)
        self.eng_sem[e] += inc
        cnt = self.eng_sem[e]

        def f(eng, plan=self, emit=emit, inc=inc, e=e):
            ins = emit(eng)
            ins.then_inc(plan._sem_handles[e], inc)
        self.cl[e].append(f)
        for w in writes:
            self.lastw[w] = {e: cnt}
            self.readers[w] = {}
        for r in reads:
            self.readers.setdefault(r, {})[e] = cnt
        return cnt

    def dma(self, e, sem, reads=(), writes=(), emit=None):
        reads = [rk(r) for r in reads]
        writes = [rk(w) for w in writes]
        self._dep_waits(e, reads, writes)
        self.dma_sems[sem] += 16
        cnt = self.dma_sems[sem]

        def f(eng, plan=self, emit=emit, sem=sem):
            ins = emit(eng)
            ins.then_inc(plan._sem_handles[sem], 16)
        self.cl[e].append(f)
        for w in writes:
            self.lastw[w] = {sem: cnt}
            self.readers[w] = {}
        for r in reads:
            self.readers.setdefault(r, {})[sem] = cnt
        return cnt

    def raw(self, e, f):
        self.cl[e].append(f)

    def barrier(self):
        snap_e = dict(self.eng_sem)
        snap_d = dict(self.dma_sems)
        for e in ENGMAP:
            for s, v in snap_e.items():
                if s != e:
                    self._emit_wait(e, s, v)
            for s, v in snap_d.items():
                self._emit_wait(e, s, v)

    def switch(self, engines, index_fn, n, body):
        import copy
        for e in engines:
            self.cl[e].append(("SW_BEGIN", index_fn, n))
        snap = (dict(self.eng_sem), dict(self.dma_sems))
        lw_snap = copy.deepcopy(self.lastw)
        rd_snap = copy.deepcopy(self.readers)
        wm_snap = {e: dict(self.wm[e]) for e in ENGMAP}
        end_state = None
        end_lw, end_rd = [], []
        for i in range(n):
            if i > 0:
                for e in engines:
                    self.cl[e].append(("SW_NEXT",))
                self.eng_sem, self.dma_sems = dict(snap[0]), dict(snap[1])
                self.lastw = copy.deepcopy(lw_snap)
                self.readers = copy.deepcopy(rd_snap)
                for e in ENGMAP:
                    self.wm[e] = dict(wm_snap[e])
            body(i)
            st = (dict(self.eng_sem), dict(self.dma_sems))
            if end_state is None:
                end_state = st
            else:
                assert st == end_state, f"switch case {i}: sem deltas differ"
            end_lw.append(copy.deepcopy(self.lastw))
            end_rd.append(copy.deepcopy(self.readers))
        for e in engines:
            self.cl[e].append(("SW_END",))
        mlw, mrd = {}, {}
        for lw in end_lw:
            for k, d in lw.items():
                t = mlw.setdefault(k, {})
                for sn, c in d.items():
                    t[sn] = max(t.get(sn, 0), c)
        for rd in end_rd:
            for k, d in rd.items():
                t = mrd.setdefault(k, {})
                for sn, c in d.items():
                    t[sn] = max(t.get(sn, 0), c)
        self.lastw, self.readers = mlw, mrd
        for e in ENGMAP:
            self.wm[e] = dict(wm_snap[e])

    def run_engine(self, e, eng, nc):
        pid = None
        gens = []
        for item in self.cl[e]:
            if callable(item):
                item(eng)
            elif item[0] == "SW_BEGIN":
                if pid is None:
                    pid = nc.partition_id([ENGMAP[e]])
                _, index_fn, n = item
                g = iter(eng.Switch(index_fn(pid), n))
                gens.append(g)
                next(g)
            elif item[0] == "SW_NEXT":
                next(gens[-1])
            else:
                try:
                    next(gens[-1])
                    raise AssertionError("switch not exhausted")
                except StopIteration:
                    pass
                gens.pop()



def _gu_lay(w):
    return np.ascontiguousarray(
        w.reshape(16, 128, 3, 256).transpose(1, 2, 0, 3).reshape(128, 12288))


def _d_lay(w):
    return np.ascontiguousarray(
        w.reshape(6, 128, 16, 128).transpose(1, 2, 0, 3).reshape(128, 12288))


def _core_token_idx(c):
    if c % 2 == 0:
        return np.concatenate([np.arange(0, 512), np.arange(1536, 2048)])
    return np.concatenate([np.arange(512, 1024), np.arange(1024, 1536)])


def prep_host(inputs):
    x = np.asarray(inputs['L0_attn_hidden_states'])
    wq = np.asarray(inputs['L0_attn_Wq'])
    wk = np.asarray(inputs['L0_attn_Wk'])
    wv = np.asarray(inputs['L0_attn_Wv'])
    wo = np.asarray(inputs['L0_attn_Wo'])
    qnw = np.ascontiguousarray(np.asarray(inputs['L0_attn_q_norm_w']).reshape(128, 1)).astype(np.float32)
    knw = np.ascontiguousarray(np.asarray(inputs['L0_attn_k_norm_w']).reshape(128, 1)).astype(np.float32)
    rw = np.asarray(inputs['L0_ffn_router_w'])
    wg = np.asarray(inputs['L0_ffn_w_gate'])
    wu = np.asarray(inputs['L0_ffn_w_up'])
    wd = np.asarray(inputs['L0_ffn_w_down'])

    wffn = np.concatenate(
        [np.concatenate([_gu_lay(wg[e]), _gu_lay(wu[e]), _d_lay(wd[e])], 1)
         for e in range(NEXP)], 1)

    def head_lay(w, nh):
        return w.reshape(16, 128, nh, 128).transpose(1, 2, 0, 3).reshape(128, nh * 2048)
    wqkv = np.ascontiguousarray(
        np.concatenate([head_lay(wk, 4), head_lay(wv, 4), head_lay(wq, 32)], 1))

    wo_r = np.ascontiguousarray(
        wo.reshape(32, 128, 16, 128).transpose(1, 0, 2, 3)
          .reshape(128, 4, 8, 16, 128).transpose(0, 1, 3, 2, 4).reshape(128, 65536))

    rtw = np.ascontiguousarray(
        rw.T.reshape(16, 128, 32).transpose(1, 0, 2).reshape(128, 512)).astype(np.float32)

    ones_col = np.ones((128, 1), np.float32)
    ones_row = np.ones((1, 128), np.float32)
    identb = np.eye(128, dtype=np.float32).astype(ml_dtypes.bfloat16)
    identf = np.eye(128, dtype=np.float32)
    sel = np.zeros((32, 4096), np.float32)
    for e in range(32):
        sel[e, 128 * e:128 * (e + 1)] = 1.0
    sel = sel.astype(ml_dtypes.bfloat16)
    tri = np.zeros((128, 2048), np.float32)
    p = np.arange(128)[:, None]
    j = np.arange(512)[None, :]
    for qc in range(4):
        tri[:, 512 * qc:512 * (qc + 1)] = np.where(j > 128 * qc + p, NEG, 0.0)
    tri = tri.astype(ml_dtypes.bfloat16)

    inv_freq = 1.0 / (1e6 ** (np.arange(0, 128, 2, dtype=np.float64) / 128.0))

    per_core = []
    for c in range(NCORES):
        b = c // 2
        idx = _core_token_idx(c)
        xT = np.ascontiguousarray(x[b, idx, :].T).astype(np.float32)
        ang = idx[None, :].astype(np.float64) * inv_freq[:, None]
        ang = np.concatenate([ang, ang], 0)
        per_core.append(dict(
            xT=xT, wffn=wffn, wqkv=wqkv, wo=wo_r, rtw=rtw,
            cosT=np.cos(ang).astype(np.float32), sinT=np.sin(ang).astype(np.float32),
            tri=tri, qnw=qnw, knw=knw, ones_col=ones_col, ones_row=ones_row,
            ones_col_bf=ones_col.astype(ml_dtypes.bfloat16),
            identb=identb, identf=identf, sel=sel))
    return per_core


def reassemble(results):
    out = np.empty((4, 2048, 2048), np.float32)
    for c in range(NCORES):
        idx = _core_token_idx(c)
        out[c // 2, idx, :] = results[c]['y'].T
    return out



def build_program(phases=('ffn0', 'attn', 'ffn1'), n_exp=NEXP, n_heads=NH, dbg=()):
    nc = bass.Bass("TRN2", target_bir_lowering=False, debug=False)
    P = Plan()

    d_xT = nc.dram_tensor("xT", [D, T], F32, kind="ExternalInput").ap()
    d_wffn = nc.dram_tensor("wffn", [128, NEXP * 36864], F32, kind="ExternalInput").ap()
    d_wqkv = nc.dram_tensor("wqkv", [128, 40 * 2048], F32, kind="ExternalInput").ap()
    d_wo = nc.dram_tensor("wo", [128, 65536], F32, kind="ExternalInput").ap()
    d_rtw = nc.dram_tensor("rtw", [128, 512], F32, kind="ExternalInput").ap()
    d_cos = nc.dram_tensor("cosT", [128, T], F32, kind="ExternalInput").ap()
    d_sin = nc.dram_tensor("sinT", [128, T], F32, kind="ExternalInput").ap()
    d_tri = nc.dram_tensor("tri", [128, 2048], BF16, kind="ExternalInput").ap()
    d_qnw = nc.dram_tensor("qnw", [128, 1], F32, kind="ExternalInput").ap()
    d_knw = nc.dram_tensor("knw", [128, 1], F32, kind="ExternalInput").ap()
    d_onesc = nc.dram_tensor("ones_col", [128, 1], F32, kind="ExternalInput").ap()
    d_onesb = nc.dram_tensor("ones_col_bf", [128, 1], BF16, kind="ExternalInput").ap()
    d_onesr = nc.dram_tensor("ones_row", [1, 128], F32, kind="ExternalInput").ap()
    d_identb = nc.dram_tensor("identb", [128, 128], BF16, kind="ExternalInput").ap()
    d_identf = nc.dram_tensor("identf", [128, 128], F32, kind="ExternalInput").ap()
    d_sel = nc.dram_tensor("sel", [32, 4096], BF16, kind="ExternalInput").ap()
    d_y = nc.dram_tensor("y", [D, T], F32, kind="ExternalOutput").ap()
    d_cc_in = nc.dram_tensor("cc_in", [128, 8192], BF16, kind="Internal").ap()
    d_cc_out = nc.dram_tensor("cc_out", [1024, 8192], BF16, kind="Internal",
                              addr_space="Shared").ap()
    d_dbg = {}
    for (name, shape, dt) in dbg:
        d_dbg[name] = nc.dram_tensor("dbg_" + name, list(shape), dt,
                                     kind="ExternalOutput").ap()

    for s in ['sConst', 'sX', 'sG0', 'sG1', 'sU0', 'sU1', 'sD0', 'sD1', 'sD2',
              'sCI', 'sKV0', 'sKV1', 'sKV2', 'sKV3', 'sY', 'sSel', 'sRope', 'sCC']:
        P.new_dma_sem(s)

    with contextlib.ExitStack() as top:
        EE = top.enter_context
        x = [EE(nc.sbuf_tensor(f"x{m}", [128, T], F32)) for m in range(KD)]
        xn = [EE(nc.sbuf_tensor(f"xn{m}", [128, T], BF16)) for m in range(KD)]
        G = [EE(nc.sbuf_tensor(f"Gw{i}", [128, 4096], BF16)) for i in range(2)]
        onesc = EE(nc.sbuf_tensor("onesc", [128, 1], F32))
        onesb = EE(nc.sbuf_tensor("onesb", [128, 1], BF16))
        onesr = EE(nc.sbuf_tensor("onesr", [1, 128], F32))
        identb = EE(nc.sbuf_tensor("identb_s", [128, 128], BF16))
        identf = EE(nc.sbuf_tensor("identf_s", [128, 128], F32))
        qnw = EE(nc.sbuf_tensor("qnw_s", [128, 1], F32))
        qnws = EE(nc.sbuf_tensor("qnw_sc", [128, 1], F32))
        knw = EE(nc.sbuf_tensor("knw_s", [128, 1], F32))
        rtw = EE(nc.sbuf_tensor("rtw_s", [128, 512], F32))

        for (tile, src) in [(onesc, d_onesc), (onesb, d_onesb), (onesr, d_onesr), (identb, d_identb),
                            (identf, d_identf), (qnw, d_qnw),
                            (knw, d_knw), (rtw, d_rtw)]:
            P.dma('S', 'sConst', writes=[tile],
                  emit=lambda eng, t=tile, s=src: eng.dma_start(t[:, :], s[:, :]))
        P.op('V', reads=[qnw], writes=[qnws],
             emit=lambda eng: eng.tensor_scalar(qnws[:, 0:1], qnw[:, 0:1], SCALING, None, ALU.mult))
        for m in range(KD):
            P.dma('S', 'sX', writes=[x[m]],
                  emit=lambda eng, m=m: eng.dma_start(x[m][:, :], d_xT[128 * m:128 * (m + 1), :]))

        def ln_sublayer(sc, with_router):
            sqb, xs32, Ab, Bb, rows = sc['sqb'], sc['xs32'], sc['Ab'], sc['Bb'], sc['rows']
            psS, psB = sc['psS'], sc['psB']
            mean_r, ve_r, rstd_r, t_r = rows
            sum_r, ssq_r, sr_r = mean_r, ve_r, t_r
            for m in range(KD):
                P.op('A', reads=[x[m]], writes=[xn[m]],
                     emit=lambda eng, m=m: eng.copy(xn[m][:, :], x[m][:, :]))
                P.op('A', reads=[xn[m]], writes=[sqb[m % 2]],
                     emit=lambda eng, m=m: eng.activation(sqb[m % 2][:, :], xn[m][:, :], AF.Square))
                for th in range(2):
                    P.op('T', reads=[xn[m], onesb], writes=[psS[th]],
                         emit=lambda eng, m=m, th=th: eng.matmul(
                             psS[th][0:1, :], onesb[:, :], xn[m][:, TH * th:TH * (th + 1)],
                             start=(m == 0), stop=(m == KD - 1)))
                    P.op('T', reads=[sqb[m % 2], onesb], writes=[psS[2 + th]],
                         emit=lambda eng, m=m, th=th: eng.matmul(
                             psS[2 + th][0:1, :], onesb[:, :], sqb[m % 2][:, TH * th:TH * (th + 1)],
                             start=(m == 0), stop=(m == KD - 1)))
            for th in range(2):
                P.op('A', reads=[psS[th]], writes=[sum_r],
                     emit=lambda eng, th=th: eng.copy(sum_r[0:1, TH * th:TH * (th + 1)], psS[th][0:1, :]))
                P.op('A', reads=[psS[2 + th]], writes=[ssq_r],
                     emit=lambda eng, th=th: eng.copy(ssq_r[0:1, TH * th:TH * (th + 1)], psS[2 + th][0:1, :]))
            P.op('V', reads=[sum_r], writes=[mean_r],
                 emit=lambda eng: eng.tensor_scalar(mean_r[0:1, :], sum_r[0:1, :], 1.0 / D, None, ALU.mult))
            P.op('V', reads=[mean_r], writes=[t_r],
                 emit=lambda eng: eng.tensor_tensor(t_r[0:1, :], mean_r[0:1, :], mean_r[0:1, :], ALU.mult))
            P.op('V', reads=[ssq_r], writes=[ve_r],
                 emit=lambda eng: eng.tensor_scalar(ve_r[0:1, :], ssq_r[0:1, :], 1.0 / D, EPS_LN, ALU.mult, ALU.add))
            P.op('V', reads=[ve_r, t_r], writes=[ve_r],
                 emit=lambda eng: eng.tensor_tensor(ve_r[0:1, :], ve_r[0:1, :], t_r[0:1, :], ALU.subtract))
            rsqrt_row(ve_r, rstd_r, t_r)
            P.op('V', reads=[mean_r, rstd_r], writes=[mean_r],
                 emit=lambda eng: eng.tensor_tensor(mean_r[0:1, :], mean_r[0:1, :], rstd_r[0:1, :], ALU.mult))
            for th in range(2):
                P.op('T', reads=[rstd_r, onesr], writes=[psB[0]],
                     emit=lambda eng, th=th: eng.matmul(psB[0][:, :], onesr[0:1, :],
                                                        rstd_r[0:1, TH * th:TH * (th + 1)], start=True, stop=True))
                P.op('A', reads=[psB[0]], writes=[Ab],
                     emit=lambda eng, th=th: eng.copy(Ab[:, TH * th:TH * (th + 1)], psB[0][:, :]))
                P.op('T', reads=[mean_r, onesr], writes=[psB[1]],
                     emit=lambda eng, th=th: eng.matmul(psB[1][:, :], onesr[0:1, :],
                                                        mean_r[0:1, TH * th:TH * (th + 1)], start=True, stop=True))
                P.op('A', reads=[psB[1]], writes=[Bb],
                     emit=lambda eng, th=th: eng.copy(Bb[:, TH * th:TH * (th + 1)], psB[1][:, :]))
            for m in range(KD):
                pp = m % 2
                P.op('V', reads=[x[m], Ab], writes=[xs32[pp]],
                     emit=lambda eng, m=m, pp=pp: eng.tensor_tensor(
                         xs32[pp][:, :], x[m][:, :], Ab[:, :], ALU.mult))
                if with_router:
                    P.op('V', reads=[xs32[pp], Bb], writes=[xs32[pp]],
                         emit=lambda eng, pp=pp: eng.tensor_tensor(
                             xs32[pp][:, :], xs32[pp][:, :], Bb[:, :], ALU.subtract))
                    P.op('A', reads=[xs32[pp]], writes=[xn[m]],
                         emit=lambda eng, m=m, pp=pp: eng.copy(xn[m][:, :], xs32[pp][:, :]))
                    psL = sc['psL']
                    for th in range(2):
                        P.op('T', reads=[xs32[pp], rtw], writes=[psL[th]],
                             emit=lambda eng, m=m, pp=pp, th=th: eng.matmul(
                                 psL[th][0:32, :], rtw[:, 32 * m:32 * (m + 1)],
                                 xs32[pp][:, TH * th:TH * (th + 1)],
                                 start=(m == 0), stop=(m == KD - 1)))
                else:
                    P.op('V', reads=[xs32[pp], Bb], writes=[xn[m]],
                         emit=lambda eng, m=m, pp=pp: eng.tensor_tensor(
                             xn[m][:, :], xs32[pp][:, :], Bb[:, :], ALU.subtract))

        def rsqrt_row(ve_r, rstd_r, t_r, n=T):
            P.op('A', reads=[ve_r], writes=[t_r],
                 emit=lambda eng: eng.activation(t_r[0:1, :n], ve_r[0:1, :n], AF.Ln))
            P.op('A', reads=[t_r], writes=[rstd_r],
                 emit=lambda eng: eng.activation(rstd_r[0:1, :n], t_r[0:1, :n], AF.Exp, scale=-0.5))

        def bc8(tile):
            h = tile[:, 0:8]
            return bass.AP(tensor=h.tensor, offset=h.offset,
                           ap=[[h.ap[0][0], 128], [1, 8], [0, 32]])

        def r3(ap):
            return ap.rearrange("p (a b) -> p a b", a=8)

        def router_topk_part1(sc):
            lg, ltm, etm, ptk, wtm, wfmb = sc['lg'], sc['ltm'], sc['etm'], sc['ptk'], sc['wtm'], sc['wfmb']
            psL, psT = sc['psL'], sc['psB'][0]
            m8, ss8, rs8 = sc['small'][0:3]
            for th in range(2):
                P.op('A', reads=[psL[th]], writes=[lg],
                     emit=lambda eng, th=th: eng.copy(lg[0:32, TH * th:TH * (th + 1)], psL[th][0:32, :]))
            for c in range(8):
                P.op('T', reads=[lg, identf], writes=[psT],
                     emit=lambda eng, c=c: eng.transpose(psT[:, 0:32], lg[0:32, 128 * c:128 * (c + 1)],
                                                         identf[0:32, 0:32]))
                P.op('A', reads=[psT], writes=[ltm],
                     emit=lambda eng, c=c: eng.copy(ltm[:, 32 * c:32 * (c + 1)], psT[:, 0:32]))
            P.op('V', reads=[ltm], writes=[m8],
                 emit=lambda eng: eng.reduce_max(m8[:, 0:8], r3(ltm[:, :]), axis=AX.X))
            P.op('V', reads=[ltm, m8], writes=[ptk],
                 emit=lambda eng: eng.tensor_tensor(r3(ptk[:, :]), r3(ltm[:, :]), bc8(m8), ALU.subtract))
            P.op('A', reads=[ptk], writes=[etm],
                 emit=lambda eng: eng.activation(etm[:, :], ptk[:, :], AF.Exp))
            P.op('A', reads=[etm], writes=[ptk],
                 emit=lambda eng: eng.copy(ptk[:, :], etm[:, :]))
            P.op('V', writes=[ss8],
                 emit=lambda eng: eng.memset(ss8[:, 0:8], 0.0))
            for i in range(8):
                P.op('V', reads=[ptk], writes=[m8],
                     emit=lambda eng: eng.reduce_max(m8[:, 0:8], r3(ptk[:, :]), axis=AX.X))
                P.op('V', reads=[m8, ss8], writes=[ss8],
                     emit=lambda eng: eng.tensor_tensor(ss8[:, 0:8], ss8[:, 0:8], m8[:, 0:8], ALU.add))
                if i < 7:
                    P.op('V', reads=[ptk, m8], writes=[ltm],
                         emit=lambda eng: eng.tensor_tensor(r3(ltm[:, :]), r3(ptk[:, :]), bc8(m8), ALU.is_ge))
                    P.op('V', reads=[ltm, ptk], writes=[wtm],
                         emit=lambda eng: eng.tensor_tensor(wtm[:, :], ltm[:, :], ptk[:, :], ALU.mult))
                    P.op('V', reads=[wtm, ptk], writes=[ptk],
                         emit=lambda eng: eng.tensor_tensor(ptk[:, :], ptk[:, :], wtm[:, :], ALU.subtract))
            P.op('V', reads=[ss8], writes=[rs8],
                 emit=lambda eng: eng.reciprocal(rs8[:, 0:8], ss8[:, 0:8]))
            P.op('V', reads=[rs8], writes=[rs8],
                 emit=lambda eng: eng.tensor_scalar(rs8[:, 0:8], rs8[:, 0:8], 0.5, None, ALU.mult))
            P.op('V', reads=[etm, m8], writes=[ltm],
                 emit=lambda eng: eng.tensor_tensor(r3(ltm[:, :]), r3(etm[:, :]), bc8(m8), ALU.is_ge))
            P.op('V', reads=[ltm, etm], writes=[wtm],
                 emit=lambda eng: eng.tensor_tensor(wtm[:, :], etm[:, :], ltm[:, :], ALU.mult))
            P.op('V', reads=[wtm, rs8], writes=[wtm],
                 emit=lambda eng: eng.tensor_tensor(r3(wtm[:, :]), r3(wtm[:, :]), bc8(rs8), ALU.mult))

        def router_topk_part2(sc):
            wtm, wfmb, psT = sc['wtm'], sc['wfmb'], sc['psB'][0]
            for c in range(8):
                P.op('T', reads=[wtm, identf], writes=[psT],
                     emit=lambda eng, c=c: eng.transpose(psT[0:32, 0:128], wtm[:, 32 * c:32 * (c + 1)],
                                                         identf[:, :]))
                P.op('A', reads=[psT], writes=[wfmb],
                     emit=lambda eng, c=c: eng.copy(wfmb[0:32, 128 * c:128 * (c + 1)], psT[0:32, 0:128]))

        def ffn_pass(pi):
            tag = f'F{pi}_'
            with contextlib.ExitStack() as scx:
                E = scx.enter_context
                U = [E(nc.sbuf_tensor(f"{tag}U{i}", [128, 4096], BF16)) for i in range(2)]
                Dw = [E(nc.sbuf_tensor(f"{tag}D{i}", [128, 1536], BF16)) for i in range(3)]
                h = [E(nc.sbuf_tensor(f"{tag}h{i}", [128, T], BF16)) for i in range(6)]
                wbc = [E(nc.sbuf_tensor(f"{tag}wbc{i}", [128, T], BF16)) for i in range(2)]
                sel_sb = E(nc.sbuf_tensor(f"{tag}sel", [32, 4096], BF16))
                xs32 = [E(nc.sbuf_tensor(f"{tag}xs{i}", [128, T], F32)) for i in range(2)]
                Ab = E(nc.sbuf_tensor(f"{tag}Ab", [128, T], F32))
                Bb = E(nc.sbuf_tensor(f"{tag}Bb", [128, T], F32))
                rows = [E(nc.sbuf_tensor(f"{tag}row{i}", [1, T], F32)) for i in range(4)]
                lg = E(nc.sbuf_tensor(f"{tag}lg", [32, T], F32))
                ltm = E(nc.sbuf_tensor(f"{tag}ltm", [128, 256], F32))
                etm = E(nc.sbuf_tensor(f"{tag}etm", [128, 256], F32))
                ptk = E(nc.sbuf_tensor(f"{tag}ptk", [128, 256], F32))
                wtm = E(nc.sbuf_tensor(f"{tag}wtm", [128, 256], F32))
                wfmb = E(nc.sbuf_tensor(f"{tag}wfmb", [32, T], BF16))
                small = [E(nc.sbuf_tensor(f"{tag}sm{i}", [128, 32], F32)) for i in range(3)]
                psG = [E(nc.psum_tensor(f"{tag}psG{i}", [128, 512], F32)) for i in range(4)]
                psU = [E(nc.psum_tensor(f"{tag}psU{i}", [128, 512], F32)) for i in range(4)]
                sc = dict(sqb=h[0:2], xs32=xs32, Ab=Ab, Bb=Bb, rows=rows, lg=lg, ltm=ltm,
                          etm=etm, ptk=ptk, wtm=wtm, wfmb=wfmb, small=small,
                          psS=psG, psB=psU[:2], psL=psU[2:4])

                P.dma('S', 'sSel', writes=[sel_sb],
                      emit=lambda eng: eng.dma_start(sel_sb[:, :], d_sel[:, :]))

                loadq = []
                lidx = {}

                def push_load(key, buf, sem, lo, hi):
                    lidx[key] = len(loadq)

                    def planit():
                        P.dma('P', sem, writes=[buf],
                              emit=lambda eng: eng.dma_start(buf[:, 0:hi - lo], d_wffn[:, lo:hi]))
                    loadq.append(planit)

                def push_G(e, g):
                    sl = (3 * e + g) % 2
                    base = 36864 * e
                    push_load(('G', e, g), G[sl], f'sG{sl}', base + 4096 * g, base + 4096 * (g + 1))

                def push_U(e, g):
                    sl = (3 * e + g) % 2
                    base = 36864 * e
                    push_load(('U', e, g), U[sl], f'sU{sl}',
                              base + 12288 + 4096 * g, base + 12288 + 4096 * (g + 1))

                def push_D(e, q):
                    sl = (8 * e + q) % 3
                    base = 36864 * e
                    push_load(('D', e, q), Dw[sl], f'sD{sl}',
                              base + 24576 + 1536 * q, base + 24576 + 1536 * (q + 1))

                push_G(0, 0); push_U(0, 0); push_G(0, 1); push_U(0, 1)
                push_G(0, 2); push_U(0, 2)
                for e in range(n_exp):
                    if e + 1 < n_exp:
                        push_G(e + 1, 0)
                    for q in range(8):
                        push_D(e, q)
                    if e + 1 < n_exp:
                        push_U(e + 1, 0)
                        for g in (1, 2):
                            push_G(e + 1, g)
                            push_U(e + 1, g)
                pos = [0]

                def pump(n):
                    n = min(n, len(loadq))
                    while pos[0] < n:
                        loadq[pos[0]]()
                        pos[0] += 1

                pump(4)
                ln_sublayer(sc, True)
                router_topk_part1(sc)

                def gate_groups(e, g):
                    P1 = bank1(e)
                    sl = (3 * e + g) % 2
                    pump(lidx[('G', e, g)] + 3)
                    for b in range(4):
                        mh, th = b // 2, b % 2

                        def em(eng, sl=sl, P1=P1, mh=mh, th=th):
                            for k in range(16):
                                mm = eng.matmul(P1[2 * mh + th][:, :],
                                                G[sl][:, 256 * k + 128 * mh:256 * k + 128 * (mh + 1)],
                                                xn[k][:, TH * th:TH * (th + 1)],
                                                start=(k == 0), stop=(k == 15))
                            return mm
                        P.op('T', reads=[G[sl]] + xn, writes=[P1[2 * mh + th]], emit=em)

                def silu_evacs(e, g):
                    P1 = bank1(e)
                    for b in range(4):
                        mh, th = b // 2, b % 2
                        hm = 2 * g + mh
                        P.op('A', reads=[P1[2 * mh + th]], writes=[h[hm]],
                             emit=lambda eng, hm=hm, mh=mh, th=th, P1=P1: eng.activation(
                                 h[hm][:, TH * th:TH * (th + 1)], P1[2 * mh + th][:, :], AF.Silu))

                def up_groups(e, g):
                    P2 = bank2(e)
                    sl = (3 * e + g) % 2
                    pump(lidx[('U', e, g)] + 3)
                    for b in range(4):
                        mh, th = b // 2, b % 2
                        hm = 2 * g + mh

                        def em(eng, sl=sl, P2=P2, mh=mh, th=th):
                            for k in range(16):
                                mm = eng.matmul(P2[2 * mh + th][:, :],
                                                U[sl][:, 256 * k + 128 * mh:256 * k + 128 * (mh + 1)],
                                                xn[k][:, TH * th:TH * (th + 1)],
                                                start=(k == 0), stop=(k == 15))
                            return mm
                        P.op('T', reads=[U[sl]] + xn, writes=[P2[2 * mh + th]], emit=em)
                        P.op('V', reads=[P2[2 * mh + th], h[hm]], writes=[h[hm]],
                             emit=lambda eng, hm=hm, mh=mh, th=th, P2=P2: eng.tensor_tensor(
                                 h[hm][:, TH * th:TH * (th + 1)], P2[2 * mh + th][:, :],
                                 h[hm][:, TH * th:TH * (th + 1)], ALU.mult))
                        if e > 0 and th == 1:
                            P.op('P', reads=[h[hm], wbc[e % 2]], writes=[h[hm]],
                                 emit=lambda eng, hm=hm, e=e: eng.tensor_tensor(
                                     h[hm][:, :], h[hm][:, :], wbc[e % 2][:, :], ALU.mult))

                def bank1(e):
                    return psG if e % 2 == 0 else psU

                def bank2(e):
                    return psU if e % 2 == 0 else psG

                def wbc_mm(e):
                    b = e % 2
                    pt = bank1(e)
                    for th in range(2):
                        P.op('T', reads=[wfmb, sel_sb], writes=[pt[th]],
                             emit=lambda eng, e=e, th=th, pt=pt: eng.matmul(
                                 pt[th][:, :], sel_sb[0:32, 128 * e:128 * (e + 1)],
                                 wfmb[0:32, TH * th:TH * (th + 1)], start=True, stop=True))
                        P.op('A', reads=[pt[th]], writes=[wbc[b]],
                             emit=lambda eng, b=b, th=th, pt=pt: eng.copy(wbc[b][:, TH * th:TH * (th + 1)], pt[th][:, :]))

                gate_groups(0, 0)
                for e in range(n_exp):
                    P1 = bank1(e)
                    for g in range(3):
                        if g > 0:
                            gate_groups(e, g)
                        silu_evacs(e, g)
                        up_groups(e, g)
                    if e == 0:
                        router_topk_part2(sc)
                        wbc_mm(0)
                        for hm in range(6):
                            P.op('P', reads=[h[hm], wbc[0]], writes=[h[hm]],
                                 emit=lambda eng, hm=hm: eng.tensor_tensor(
                                     h[hm][:, :], h[hm][:, :], wbc[0][:, :], ALU.mult))
                    if e + 1 < n_exp:
                        wbc_mm(e + 1)
                        gate_groups(e + 1, 0)
                    for m2 in range(16):
                        pump(lidx[('D', e, m2 // 2)] + 3)
                        sl = (8 * e + m2 // 2) % 3

                        def em_down(eng, m2=m2, sl=sl, P1=P1):
                            for k in range(6):
                                for th in range(2):
                                    mm = eng.matmul(P1[2 * (m2 % 2) + th][:, :],
                                                    Dw[sl][:, 768 * (m2 % 2) + 128 * k:768 * (m2 % 2) + 128 * (k + 1)],
                                                    h[k][:, TH * th:TH * (th + 1)],
                                                    start=(k == 0), stop=(k == 5))
                            return mm
                        P.op('T', reads=[Dw[sl]] + h, writes=[P1[2 * (m2 % 2)], P1[2 * (m2 % 2) + 1]],
                             emit=em_down)
                        for th in range(2):
                            P.op('V', reads=[P1[2 * (m2 % 2) + th], x[m2]], writes=[x[m2]],
                                 emit=lambda eng, m2=m2, th=th, P1=P1: eng.tensor_tensor(
                                     x[m2][:, TH * th:TH * (th + 1)], P1[2 * (m2 % 2) + th][:, :],
                                     x[m2][:, TH * th:TH * (th + 1)], ALU.add))
                for name in list(d_dbg):
                    if name.startswith(f'ffn{pi}'):
                        dump_dbg(name)
                P.barrier()

        def dump_dbg(name):
            tgt = name.split('_', 1)[1]
            src = {'x': x, 'xn': xn}[tgt]
            for m in range(KD):
                P.dma('S', 'sY', reads=[src[m]],
                      emit=lambda eng, m=m, name=name: eng.dma_start(
                          d_dbg[name][128 * m:128 * (m + 1), :], src[m][:, :]))

        def attn_pass():
            tag = 'AT_'
            with contextlib.ExitStack() as lsc:
                E = lsc.enter_context
                xs2 = [E(nc.sbuf_tensor(f"{tag}lxs{i}", [128, T], F32)) for i in range(2)]
                sq2 = [E(nc.sbuf_tensor(f"{tag}lsq{i}", [128, T], BF16)) for i in range(2)]
                Ab2 = E(nc.sbuf_tensor(f"{tag}lAb", [128, T], F32))
                Bb2 = E(nc.sbuf_tensor(f"{tag}lBb", [128, T], F32))
                lrows = [E(nc.sbuf_tensor(f"{tag}lrow{i}", [1, T], F32)) for i in range(4)]
                psS = [E(nc.psum_tensor(f"{tag}lpsS{i}", [128, 512], F32)) for i in range(4)]
                psB = [E(nc.psum_tensor(f"{tag}lpsB{i}", [128, 512], F32)) for i in range(2)]
                ln_sublayer(dict(sqb=sq2, xs32=xs2, Ab=Ab2, Bb=Bb2, rows=lrows,
                                 psS=psS, psB=psB, psL=None), False)
            P.barrier()

            with contextlib.ExitStack() as scx:
                E = scx.enter_context
                cosT = E(nc.sbuf_tensor(f"{tag}cos", [128, T], F32))
                sinT = E(nc.sbuf_tensor(f"{tag}sin", [128, T], F32))
                sqA = E(nc.sbuf_tensor(f"{tag}sq", [128, T], F32))
                rb_sb = E(nc.sbuf_tensor(f"{tag}rb", [128, TH], F32))
                rows = [E(nc.sbuf_tensor(f"{tag}row{i}", [1, T], F32)) for i in range(3)]
                tri_s = E(nc.sbuf_tensor(f"{tag}tri", [128, 2048], BF16))
                kb = E(nc.sbuf_tensor(f"{tag}kb", [128, T], BF16))
                vb = E(nc.sbuf_tensor(f"{tag}vb", [128, T], BF16))
                vt = E(nc.sbuf_tensor(f"{tag}vt", [128, T], BF16))
                K = [E(nc.sbuf_tensor(f"{tag}K{g}", [128, 2048], BF16)) for g in range(4)]
                V = [E(nc.sbuf_tensor(f"{tag}V{g}", [128, 2048], BF16)) for g in range(4)]
                QO = [E(nc.sbuf_tensor(f"{tag}QO{i}", [128, T], BF16)) for i in range(8)]
                e_sb = E(nc.sbuf_tensor(f"{tag}esb", [128, 2048], BF16))
                eTq = E(nc.sbuf_tensor(f"{tag}eTq", [128, 2048], BF16))
                ssum = E(nc.sbuf_tensor(f"{tag}ssum", [128, 4], F32))
                rsc = E(nc.sbuf_tensor(f"{tag}rsc", [128, 4], F32))
                nmx = E(nc.sbuf_tensor(f"{tag}nmx", [128, 1], F32))
                bia = E(nc.sbuf_tensor(f"{tag}bia", [128, 1], F32))
                pscore = E(nc.psum_tensor(f"{tag}pscore", [128, 2048], F32))
                ptp = E(nc.psum_tensor(f"{tag}ptp", [128, 1024], BF16))
                psO = [E(nc.psum_tensor(f"{tag}psO{i}", [128, 512], F32)) for i in range(2)]
                psx = E(nc.psum_tensor(f"{tag}psx", [128, 512], F32))
                ve_r, rstd_r, t_r = rows

                P.dma('S', 'sRope', writes=[cosT],
                      emit=lambda eng: eng.dma_start(cosT[:, :], d_cos[:, :]))
                P.dma('S', 'sRope', writes=[sinT],
                      emit=lambda eng: eng.dma_start(sinT[:, :], d_sin[:, :]))
                P.dma('S', 'sRope', writes=[tri_s],
                      emit=lambda eng: eng.dma_start(tri_s[:, :], d_tri[:, :]))

                loadq = []

                def push_wload(lo, hi, src=d_wqkv):
                    i = len(loadq)
                    sl = i % 2

                    def planit(sl=sl, lo=lo, hi=hi, src=src):
                        P.dma('P', f'sG{sl}', writes=[G[sl]],
                              emit=lambda eng: eng.dma_start(G[sl][:, 0:hi - lo], src[:, lo:hi]))
                    loadq.append(planit)
                    return i, sl

                ld_ids = {}
                for hk in range(4):
                    ld_ids[('k', hk)] = push_wload(2048 * hk, 2048 * (hk + 1))
                for hk in range(4):
                    ld_ids[('v', hk)] = push_wload(8192 + 2048 * hk, 8192 + 2048 * (hk + 1))
                for w in range(n_heads // 8):
                    for i in range(8):
                        hq = 8 * w + i
                        ld_ids[('q', hq)] = push_wload(16384 + 2048 * hq, 16384 + 2048 * (hq + 1))
                    for j4 in range(4):
                        ld_ids[('o', w, j4)] = push_wload(16384 * w + 4096 * j4,
                                                          16384 * w + 4096 * (j4 + 1), src=d_wo)
                pos = [0]

                def pump(n):
                    n = min(n, len(loadq))
                    while pos[0] < n:
                        loadq[pos[0]]()
                        pos[0] += 1

                PK = [f'{tag}pk{i}' for i in range(4)]

                def qps(par, th):
                    if par == 0:
                        return psO[th][:, :], rk(psO[th])
                    return pscore[:, 1024 + TH * th:1024 + TH * (th + 1)], PK[2 + th]

                def head_mm(sl, par):
                    aps = [qps(par, th) for th in range(2)]

                    def em(eng, sl=sl, aps=aps):
                        for k in range(16):
                            for th in range(2):
                                mm = eng.matmul(aps[th][0], G[sl][:, 128 * k:128 * (k + 1)],
                                                xn[k][:, TH * th:TH * (th + 1)],
                                                start=(k == 0), stop=(k == 15))
                        return mm
                    P.op('T', reads=[G[sl]] + xn, writes=[a[1] for a in aps], emit=em)

                def rms_rope(norm_w, dst, par, peng='P'):
                    aps = [qps(par, th) for th in range(2)]
                    for th in range(2):
                        P.op('A', reads=[aps[th][1]], writes=[vb],
                             emit=lambda eng, th=th, aps=aps: eng.activation(
                                 vb[:, TH * th:TH * (th + 1)], aps[th][0], AF.Square))
                        P.op('T', reads=[vb, onesb], writes=[PK[th]],
                             emit=lambda eng, th=th: eng.matmul(
                                 pscore[0:1, TH * th:TH * (th + 1)], onesb[:, :],
                                 vb[:, TH * th:TH * (th + 1)], start=True, stop=True))
                    P.op('V', reads=[PK[0], PK[1]], writes=[ve_r],
                         emit=lambda eng: eng.tensor_scalar(ve_r[0:1, :], pscore[0:1, 0:T],
                                                            1.0 / HD, EPS_RMS, ALU.mult, ALU.add))
                    rsqrt_row(ve_r, rstd_r, t_r)
                    for th in range(2):
                        P.op('T', reads=[rstd_r, onesr], writes=[psx],
                             emit=lambda eng, th=th: eng.matmul(psx[:, :], onesr[0:1, :],
                                                                rstd_r[0:1, TH * th:TH * (th + 1)],
                                                                start=True, stop=True))
                        P.op('A', reads=[psx], writes=[rb_sb],
                             emit=lambda eng, th=th: eng.copy(rb_sb[:, :], psx[:, :]))
                        P.op('V', reads=[aps[th][1], norm_w, rb_sb], writes=[PK[th]],
                             emit=lambda eng, th=th, aps=aps: eng.scalar_tensor_tensor(
                                 pscore[:, TH * th:TH * (th + 1)], aps[th][0], norm_w[:, 0:1],
                                 rb_sb[:, :], ALU.mult, ALU.mult))
                    so = slice(0, T)
                    kq = pscore
                    P.op('V', reads=[PK[0], PK[1], cosT], writes=[sqA],
                         emit=lambda eng: eng.tensor_tensor(sqA[0:64, :], kq[0:64, 0:T], cosT[0:64, :], ALU.mult))
                    P.op('V', reads=[PK[0], PK[1], sinT], writes=[dst],
                         emit=lambda eng: eng.scalar_tensor_tensor(
                             dst[0:64, so], kq[64:128, 0:T], -1.0, sinT[0:64, :], ALU.mult, ALU.mult))
                    P.op(peng, reads=[dst, sqA], writes=[dst],
                         emit=lambda eng: eng.tensor_tensor(dst[0:64, so], dst[0:64, so], sqA[0:64, :], ALU.add))
                    P.op('V', reads=[PK[0], PK[1], cosT], writes=[sqA],
                         emit=lambda eng: eng.tensor_tensor(sqA[64:128, :], kq[64:128, 0:T], cosT[64:128, :], ALU.mult))
                    P.op('V', reads=[PK[0], PK[1], sinT], writes=[dst],
                         emit=lambda eng: eng.tensor_tensor(
                             dst[64:128, so], kq[0:64, 0:T], sinT[64:128, :], ALU.mult))
                    P.op(peng, reads=[dst, sqA], writes=[dst],
                         emit=lambda eng: eng.tensor_tensor(dst[64:128, so], dst[64:128, so], sqA[64:128, :], ALU.add))

                for hk in range(4):
                    i, sl = ld_ids[('k', hk)]
                    pump(i + 2)
                    head_mm(sl, hk % 2)
                    rms_rope(knw, kb, hk % 2)
                    P.dma('S', 'sCI', reads=[kb], writes=['cc_in'],
                          emit=lambda eng, hk=hk: eng.dma_start(
                              d_cc_in[:, 1024 * hk:1024 * (hk + 1)], kb[:, :]))
                for hk in range(4):
                    i, sl = ld_ids[('v', hk)]
                    pump(i + 2)
                    head_mm(sl, hk % 2)
                    aps = [qps(hk % 2, th) for th in range(2)]
                    for th in range(2):
                        P.op('A', reads=[aps[th][1]], writes=[vb],
                             emit=lambda eng, th=th, aps=aps: eng.copy(vb[:, TH * th:TH * (th + 1)], aps[th][0]))

                    def em_vt(eng):
                        for i2 in range(8):
                            mm = eng.transpose(ptp[:, 128 * i2:128 * (i2 + 1)],
                                               vb[:, 128 * i2:128 * (i2 + 1)], identb[:, :])
                        return mm
                    P.op('T', reads=[vb, identb], writes=['ptpB'], emit=em_vt)
                    P.op('V', reads=['ptpB'], writes=[vt],
                         emit=lambda eng: eng.tensor_copy(vt[:, :], ptp[:, :]))
                    P.dma('S', 'sCI', reads=[vt], writes=['cc_in'],
                          emit=lambda eng, hk=hk: eng.dma_start(
                              d_cc_in[:, 4096 + 1024 * hk:4096 + 1024 * (hk + 1)], vt[:, :]))

                P.op('P', reads=['cc_in'], writes=['cc_out'], inc=1,
                     emit=lambda eng: eng.collective_compute(
                         "AllGather", ALU.bypass,
                         replica_groups=[[0, 1, 2, 3, 4, 5, 6, 7]],
                         ins=[d_cc_in[:, :]], outs=[d_cc_out[:, :]]))

                def rb_body(pair):
                    r0 = 256 * pair
                    r1 = 256 * pair + 128
                    for g in range(4):
                        for (dstc, rr, sc0) in [(0, r0, 0), (512, r1, 0), (1024, r1, 512), (1536, r0, 512)]:
                            P.dma('P', f'sKV{g}', reads=['cc_out'], writes=[K[g]],
                                  emit=lambda eng, g=g, dstc=dstc, rr=rr, sc0=sc0: eng.dma_start(
                                      K[g][:, dstc:dstc + 512],
                                      d_cc_out[rr:rr + 128, 1024 * g + sc0:1024 * g + sc0 + 512]))
                        for (dstc, rr, sc0) in [(0, r0, 0), (512, r1, 0), (1024, r1, 512), (1536, r0, 512)]:
                            P.dma('P', f'sKV{g}', reads=['cc_out'], writes=[V[g]],
                                  emit=lambda eng, g=g, dstc=dstc, rr=rr, sc0=sc0: eng.dma_start(
                                      V[g][:, dstc:dstc + 512],
                                      d_cc_out[rr:rr + 128, 4096 + 1024 * g + sc0:4096 + 1024 * g + sc0 + 512]))
                P.switch(['P'], lambda pid: pid // 2, 4, rb_body)

                def attn_head(par, hq, slot):
                    g = hq // 8
                    blocks = ([(0, 512, 0), (512, 2048, 1536)] if par == 0
                              else [(0, 1024, 512), (512, 1536, 1024)])
                    for b_idx, (qoff, ktl, dbase) in enumerate(blocks):
                        nkc = ktl // 128
                        pend = [None]
                        for qc in range(4):
                            roff = (1024 * (qc % 2)) if b_idx == 0 else 0
                            kys = PK[roff // 512:(roff + ktl + 511) // 512]

                            def emit_sc(qoff=qoff, qc=qc, ktl=ktl, g=g, slot=slot, roff=roff, kys=kys):
                                def em_sc(eng, qoff=qoff, qc=qc, ktl=ktl, g=g, slot=slot, roff=roff):
                                    for kc5 in range(ktl // 512):
                                        mm = eng.matmul(pscore[:, roff + 512 * kc5:roff + 512 * (kc5 + 1)],
                                                        QO[slot][:, qoff + 128 * qc:qoff + 128 * (qc + 1)],
                                                        K[g][:, 512 * kc5:512 * (kc5 + 1)],
                                                        start=True, stop=True)
                                    return mm
                                P.op('T', reads=[QO[slot], K[g]], writes=kys, emit=em_sc)
                            if pend[0]:
                                pend[0](emit_sc)
                                pend[0] = None
                            else:
                                emit_sc()
                            P.op('V', reads=kys + [tri_s], writes=kys,
                                 emit=lambda eng, dbase=dbase, qc=qc, roff=roff: eng.tensor_tensor(
                                     pscore[:, roff + dbase:roff + dbase + 512],
                                     pscore[:, roff + dbase:roff + dbase + 512],
                                     tri_s[:, 512 * qc:512 * (qc + 1)], ALU.add))
                            P.op('V', reads=kys, writes=[nmx],
                                 emit=lambda eng, ktl=ktl, roff=roff: eng.reduce_max(
                                     nmx[:, 0:1], pscore[:, roff:roff + ktl], axis=AX.X, negate=True))
                            P.op('A', reads=kys + [nmx], writes=[e_sb, ssum],
                                 emit=lambda eng, ktl=ktl, qc=qc, roff=roff: eng.activation(
                                     e_sb[:, 0:ktl], pscore[:, roff:roff + ktl], AF.Exp, bias=nmx[:, 0:1],
                                     scale=1.0, accum_out=ssum[:, qc:qc + 1]))

                            def mk_pend(qc=qc, nkc=nkc, g=g, b_idx=b_idx):
                                def tp_half(hlo, hhi):
                                    def em_tp(eng, hlo=hlo, hhi=hhi):
                                        for i2 in range(hlo, hhi):
                                            mm = eng.transpose(ptp[:, 128 * (i2 - hlo):128 * (i2 - hlo + 1)],
                                                               e_sb[:, 128 * i2:128 * (i2 + 1)], identb[:, :])
                                        return mm
                                    P.op('T', reads=[e_sb, identb], writes=['ptpB'], emit=em_tp)

                                def cp_sub(hlo, lo, hi):
                                    key = f'eTqS{lo // 4}'
                                    P.op('V', reads=['ptpB'], writes=[key],
                                         emit=lambda eng, hlo=hlo, lo=lo, hi=hi: eng.tensor_copy(
                                             eTq[:, 128 * lo:128 * hi],
                                             ptp[:, 128 * (lo - hlo):128 * (hi - hlo)]))
                                    return key

                                def av_sub(key, lo, hi):
                                    def em_av(eng, lo=lo, hi=hi, g=g, b_idx=b_idx, qc=qc, nkc=nkc):
                                        for i2 in range(lo, hi):
                                            mm = eng.matmul(psO[b_idx][:, 128 * qc:128 * (qc + 1)],
                                                            V[g][:, 128 * i2:128 * (i2 + 1)],
                                                            eTq[:, 128 * i2:128 * (i2 + 1)],
                                                            start=(i2 == 0), stop=(i2 == nkc - 1))
                                        return mm
                                    P.op('T', reads=[key, V[g]], writes=[rk(psO[b_idx])], emit=em_av)

                                def pendf(emit_sc2):
                                    first = True
                                    for hlo in range(0, nkc, 8):
                                        hhi = min(nkc, hlo + 8)
                                        tp_half(hlo, hhi)
                                        if first and emit_sc2:
                                            emit_sc2()
                                        first = False
                                        for lo in range(hlo, hhi, 4):
                                            hi = min(hhi, lo + 4)
                                            av_sub(cp_sub(hlo, lo, hi), lo, hi)
                                return pendf
                            pend[0] = mk_pend()
                        pend[0](None)
                        P.op('V', reads=[ssum], writes=[rsc],
                             emit=lambda eng: eng.reciprocal(rsc[:, 0:4], ssum[:, 0:4]))

                        def em_rst(eng):
                            for qc in range(4):
                                mm = eng.transpose(psx[0:1, 128 * qc:128 * (qc + 1)],
                                                   rsc[:, qc:qc + 1], identf[:, :])
                            return mm
                        P.op('T', reads=[rsc, identf], writes=[psx], emit=em_rst)
                        P.op('A', reads=[psx], writes=[t_r],
                             emit=lambda eng: eng.copy(t_r[0:1, 0:512], psx[0:1, 0:512]))
                        P.op('T', reads=[t_r, onesr], writes=[psx],
                             emit=lambda eng: eng.matmul(psx[:, :], onesr[0:1, :], t_r[0:1, 0:512],
                                                         start=True, stop=True))
                        P.op('A', reads=[psx], writes=[rb_sb],
                             emit=lambda eng: eng.copy(rb_sb[:, :], psx[:, :]))
                        P.op('V', reads=[rk(psO[b_idx]), rb_sb], writes=[QO[slot]],
                             emit=lambda eng, b_idx=b_idx, qoff=qoff, slot=slot: eng.tensor_tensor(
                                 QO[slot][:, qoff:qoff + 512], psO[b_idx][:, :], rb_sb[:, :], ALU.mult))

                for w in range(n_heads // 8):
                    for i in range(8):
                        hq = 8 * w + i
                        li, sl = ld_ids[('q', hq)]
                        pump(li + 2)
                        head_mm(sl, i % 2)
                        rms_rope(qnws, QO[i], i % 2, 'V' if w == 0 else 'P')

                    def wave_attn(par, w=w):
                        for i in range(8):
                            attn_head(par, 8 * w + i, i)
                    P.switch(['T', 'V', 'A'], lambda pid: pid % 2, 2, wave_attn)

                    for m2 in range(16):
                        li, sl = ld_ids[('o', w, m2 // 4)]
                        pump(li + 2)
                        aps = [qps(m2 % 2, th) for th in range(2)]

                        def em_op(eng, sl=sl, m2=m2, aps=aps):
                            for kk in range(8):
                                for th in range(2):
                                    mm = eng.matmul(aps[th][0],
                                                    G[sl][:, 1024 * (m2 % 4) + 128 * kk:1024 * (m2 % 4) + 128 * (kk + 1)],
                                                    QO[kk][:, TH * th:TH * (th + 1)],
                                                    start=(kk == 0), stop=(kk == 7))
                            return mm
                        P.op('T', reads=[G[sl]] + QO, writes=[a[1] for a in aps], emit=em_op)
                        for th in range(2):
                            P.op('V', reads=[aps[th][1], x[m2]], writes=[x[m2]],
                                 emit=lambda eng, m2=m2, th=th, aps=aps: eng.tensor_tensor(
                                     x[m2][:, TH * th:TH * (th + 1)], aps[th][0],
                                     x[m2][:, TH * th:TH * (th + 1)], ALU.add))
                P.barrier()

        if 'ffn0' in phases:
            ffn_pass(0)
        if 'attn' in phases:
            attn_pass()
        if 'ffn1' in phases:
            ffn_pass(1)

        for m in range(KD):
            P.dma('S', 'sY', reads=[x[m]],
                  emit=lambda eng, m=m: eng.dma_start(d_y[128 * m:128 * (m + 1), :], x[m][:, :]))
        P.raw('S', lambda eng: eng.wait_ge(P._sem_handles['sY'], P.dma_sems['sY']))

        with contextlib.ExitStack() as sctx:
            SE = sctx.enter_context
            for s in ['T', 'V', 'A', 'P', 'S']:
                P._sem_handles[s] = SE(nc.semaphore(f"sem{s}"))
            for s in P.dma_sems:
                P._sem_handles[s] = SE(nc.semaphore(f"sem_{s}"))
            import os
            if os.environ.get('DUMP_SEMS'):
                for k, v in P._sem_handles.items():
                    print('SEMMAP', k, v)
            block = SE(nc.Block())

            @block.tensor
            def _(eng):
                P.run_engine('T', eng, nc)

            @block.vector
            def _(eng):
                P.run_engine('V', eng, nc)

            @block.scalar
            def _(eng):
                P.run_engine('A', eng, nc)

            @block.gpsimd
            def _(eng):
                P.run_engine('P', eng, nc)

            @block.sync
            def _(eng):
                P.run_engine('S', eng, nc)

    return nc


def kernel(**inputs):
    per_core = prep_host(inputs)
    nc = build_program()
    res = run_bass_kernel_spmd(nc, per_core, core_ids=list(range(NCORES)))
    return reassemble(res.results)
